# version 26
# baseline (speedup 1.0000x reference)
import numpy as np
import ml_dtypes
import concourse.bass as bass
import concourse.mybir as mybir
from concourse.bass_utils import run_bass_kernel_spmd
from concourse.bass_core_rust import IndirectOffsetOnAxis
from contextlib import ExitStack

F32 = mybir.dt.float32
BF16 = mybir.dt.bfloat16
AF = mybir.ActivationFunctionType
ALU = mybir.AluOpType
AX = mybir.AxisListType

N, DM, NHEAD, DH, NE, HFF = 4096, 1024, 16, 64, 8, 2048
NT, NG = 32, 8
EPS_LN = 1e-5
EPS_ATT = 1e-8

ENGS = ("sync", "scalar", "vector", "tensor", "gpsimd")


class Buf:
    __slots__ = ("ready", "free")

    def __init__(self):
        self.ready = []
        self.free = []


class Prog:

    def __init__(self):
        self.recs = {e: [] for e in ENGS}
        self.cnt = {}
        for e in ENGS:
            self.cnt["e_" + e] = 0
        self.hwm = {e: {} for e in ENGS}

    def _coalesce(self, eng, waits):
        best = {}
        for s, v in waits:
            if v > best.get(s, 0):
                best[s] = v
        out = []
        for s, v in best.items():
            assert v <= self.cnt[s], f"wait on future value of {s}: {v} > {self.cnt[s]}"
            if self.hwm[eng].get(s, 0) < v:
                self.hwm[eng][s] = v
                out.append((s, v))
        return out

    def emit(self, eng, method, *args, reads=(), writes=(), extra=(), dma=None,
             inc=True, declare_only=False, first=True, **kwargs):
        waits = list(extra)
        for b in reads:
            waits += b.ready
        if first:
            for b in writes:
                waits += b.free + b.ready
        waits = self._coalesce(eng, waits)
        if dma is not None:
            sem = "D_" + dma
            if sem not in self.cnt:
                self.cnt[sem] = 0
        else:
            sem = "e_" + eng
        if inc:
            self.cnt[sem] += 16 if dma else 1
            val = self.cnt[sem]
        self.recs[eng].append((waits, method, args, kwargs,
                               sem if inc else None, 16 if dma else 1))
        if inc:
            if not declare_only:
                for b in writes:
                    b.ready = [(sem, val)]
                    b.free = []
            for b in reads:
                b.free.append((sem, val))
        return None

    def wait_only(self, eng, extra):
        waits = self._coalesce(eng, list(extra))
        if waits:
            self.recs[eng].append((waits, None, (), {}, None, 0))

    def barrier(self, tag=""):
        targets = []
        for e in ENGS:
            self.emit(e, "nop", extra=[("e_" + e, self.cnt["e_" + e])])
            targets.append(("e_" + e, self.cnt["e_" + e]))
        for s, v in self.cnt.items():
            if s.startswith("D_"):
                targets.append((s, v))
        for e in ENGS:
            self.wait_only(e, targets)

    def realize(self, block, sems):
        for eng in ENGS:
            recs = self.recs[eng]
            if not recs:
                continue

            def body(E, recs=recs):
                for waits, method, args, kwargs, sem, n in recs:
                    for s, v in waits:
                        E.wait_ge(sems[s], v)
                    if method is None:
                        continue
                    inst = getattr(E, method)(*args, **kwargs)
                    if sem is not None:
                        inst.then_inc(sems[sem], n)
            getattr(block, eng)(body)


def build_program(debug=False):
    nc = bass.Bass("TRN2", target_bir_lowering=False)
    dram = {}

    def din(name, shape, dtype=F32):
        dram[name] = nc.dram_tensor(name, shape, dtype, kind="ExternalInput").ap()
        return dram[name]

    def dscr(name, shape, dtype):
        kind = "ExternalOutput" if debug else "Internal"
        dram[name] = nc.dram_tensor(name, shape, dtype, kind=kind).ap()
        return dram[name]

    x_in = din("x_in", [N, DM])
    wq = din("wq", [DM, DM])
    wkv = din("wkv", [DM, 2 * DM])
    wo = din("wo", [DM, DM])
    bo_b = din("bo_b", [128, DM])
    wr = din("wr", [DM, NE])
    br_b = din("br_b", [128, NE])
    w1 = din("w1", [NE, DM, HFF])
    b1_l = din("b1_l", [NE, 128, 16])
    w2 = din("w2", [NE, HFF, DM])
    b2_b = din("b2_b", [NE, 128, DM])
    maskd_in = din("maskd", [128, 4, 512], BF16)
    ident_in = din("ident", [128, 128], BF16)
    ones_in = din("ones_b", [128, 64])
    ones8_in = din("ones8", [8, 1])
    basem1_in = din("basem1", [8, 1])

    y_out = nc.dram_tensor("y_out", [N, DM], F32, kind="ExternalOutput").ap()

    q_hbm = dscr("q_hbm", [NG, 128, 8, 512], BF16)
    k_hbm = dscr("k_hbm", [8, 128, N], BF16)
    v_hbm = dscr("v_hbm", [NT, 128, 16 * 65], BF16)
    x2_hbm = dscr("x2_hbm", [N, DM], F32)
    xn2tm_hbm = dscr("xn2tm_hbm", [N, DM], BF16)
    CAP = max(256, ((N * 5 // 16) + 127) // 128 * 128)
    xs_hbm = dscr("xs_hbm", [NE * CAP + 128, DM], BF16)
    ys_hbm = dscr("ys_hbm", [NE * CAP + 128, DM], BF16)

    p = Prog()
    ctx = ExitStack()
    B_zx_holder = [None]

    sb = lambda *a, **k: ctx.enter_context(nc.sbuf_tensor(*a, **k))
    ident = sb("ident_s", [128, 128], BF16)
    maskd = sb("maskd_s", [128, 4, 512], BF16)
    ones_sb = sb("ones_sb", [128, 64], F32)
    ones_bf = sb("ones_bf", [128, 64], BF16)
    r_bf = sb("r_bf", [128, 1024], BF16)
    bo_sb = sb("bo_sb", [128, DM], F32)
    br_sb = sb("br_sb", [128, NE], F32)
    wr_bf = sb("wr_bf", [128, 8, NE], BF16)
    gates_sb = sb("gates_sb", [128, NT], F32)
    slot_sb = sb("slot_sb", [128, NT], mybir.dt.int32)
    lg_sb = sb("lg_sb", [128, 4, NE], F32)
    lmax_sb = sb("lmax_sb", [128, 4], F32)
    sub_sb = sb("sub_sb", [128, 4, NE], F32)
    sgn_sb = sb("sgn_sb", [128, 4, NE], F32)
    pe_sb = sb("pe_sb", [128, 4, NE], F32)
    den_sb = sb("den_sb", [128, 4], F32)
    gate_sb = sb("gate_sb", [128, 4], F32)

    ab_ctx = ExitStack()
    absb = lambda *a, **k: ab_ctx.enter_context(nc.sbuf_tensor(*a, **k))
    nm_sb = absb("nm_sb", [128, 4], F32)
    var_sb = absb("var_sb", [128, 4], F32)
    rstd_sb = absb("rstd_sb", [128, 4], F32)
    mr_sb = absb("mr_sb", [128, 4], F32)
    sq_sb = absb("sq_sb", [128, 2, DM], F32)
    lr_sb = absb("lr_sb", [128, 1024], F32)
    maskT_sb = absb("maskT_sb", [8, N], BF16)
    mask_bf = absb("mask_bf", [128, 4, NE], BF16)
    B_maskbf = [Buf() for _ in range(4)]

    PS01 = ctx.enter_context(nc.psum_tensor("PS01", [128, 2048], F32))
    P0 = PS01[:, 0:1024]
    P1 = PS01[:, 1024:2048]
    P2 = ctx.enter_context(nc.psum_tensor("P2", [128, 512], F32))
    P3 = ctx.enter_context(nc.psum_tensor("P3", [128, 1024], BF16))
    P4 = ctx.enter_context(nc.psum_tensor("P4", [128, 1024], F32))

    B_consts = Buf()
    B_ln = {k: [Buf() for _ in range(4)] for k in ("nm", "var", "rstd", "mr", "sq")}
    B_l = [Buf(), Buf()]
    B_r = [Buf(), Buf()]
    B_rt = {k: [Buf() for _ in range(4)] for k in
            ("lg", "lmax", "sub", "sgn", "pe", "den", "gate")}
    B_gate = [Buf() for _ in range(NT)]
    B_slot = Buf()
    B_maskT = [Buf() for _ in range(NT)]
    B_P = {id(P0): Buf(), id(P1): Buf(), id(P2): Buf(), id(P4): Buf()}
    B_P4h = [Buf(), Buf()]
    B_P3 = Buf()

    p.emit("sync", "dma_start", ident[:], ident_in, writes=[B_consts], dma="cst")
    p.emit("sync", "dma_start", maskd[:], maskd_in, writes=[B_consts], dma="cst")
    p.emit("sync", "dma_start", ones_sb[:], ones_in, writes=[B_consts], dma="cst")
    p.emit("sync", "dma_start", bo_sb[:], bo_b, writes=[B_consts], dma="cst")
    p.emit("sync", "dma_start", br_sb[:], br_b, writes=[B_consts], dma="cst")
    p.emit("vector", "tensor_copy", ones_bf[:], ones_sb[:],
           reads=[B_consts], writes=[B_consts], first=False)
    B_wr = Buf()
    p.emit("gpsimd", "dma_start", wr_bf[:],
           wr.rearrange("(dc p) e -> p dc e", p=128), writes=[B_wr], dma="wrld")

    def layer_norm(x_ap, out_ap, B_x, B_out, sl):
        nm, var, rstd = nm_sb[:, sl:sl + 1], var_sb[:, sl:sl + 1], rstd_sb[:, sl:sl + 1]
        mr = mr_sb[:, sl:sl + 1]
        sq = sq_sb[:, sl % 2, :]
        Bn, Bv, Br_, Bmr, Bsq = (B_ln["nm"][sl], B_ln["var"][sl], B_ln["rstd"][sl],
                                 B_ln["mr"][sl], B_ln["sq"][sl % 2])
        p.emit("vector", "reduce_sum", nm, x_ap, AX.X,
               reads=[B_x], writes=[Bn])
        p.emit("scalar", "activation", sq, x_ap, AF.Square, accum_out=var,
               reads=[B_x], writes=[Bv, Bsq])
        p.emit("vector", "tensor_scalar_mul", nm, nm, 1.0 / DM,
               reads=[Bn], writes=[Bn], first=False)
        p.emit("vector", "tensor_mul", mr, nm, nm,
               reads=[Bn], writes=[Bmr])
        p.emit("vector", "tensor_scalar", var, var, 1.0 / DM, EPS_LN,
               ALU.mult, ALU.add, reads=[Bv], writes=[Bv], first=False)
        p.emit("vector", "tensor_sub", var, var, mr,
               reads=[Bv, Bmr], writes=[Bv], first=False)
        p.emit("scalar", "activation", rstd, var, AF.Ln,
               reads=[Bv], writes=[Br_])
        p.emit("scalar", "activation", rstd, rstd, AF.Exp, scale=-0.5,
               reads=[Br_], writes=[Br_], first=False)
        p.emit("vector", "tensor_mul", mr, nm, rstd,
               reads=[Bn, Br_], writes=[Bmr])
        p.emit("vector", "tensor_scalar_mul", mr, mr, -1.0,
               reads=[Bmr], writes=[Bmr], first=False)
        p.emit("vector", "tensor_scalar", out_ap, x_ap, rstd, mr,
               ALU.mult, ALU.add,
               reads=[B_x, Br_, Bmr], writes=[B_out])
    tr_slot = [0]

    def transpose_tile(in_tile, out_ap, B_in, B_out):
        for dc in range(8):
            p.emit("tensor", "transpose", P3[:, dc * 128:(dc + 1) * 128],
                   in_tile[:, dc * 128:(dc + 1) * 128], ident[:],
                   reads=[B_in, B_consts], writes=[B_P3],
                   first=(dc == 0), declare_only=(dc < 7))
        p.emit("vector", "tensor_copy", out_ap, P3[:],
               reads=[B_P3], writes=[B_out])

    with ExitStack() as actx:
        asb = lambda *a: actx.enter_context(nc.sbuf_tensor(*a))
        wq_bf = asb("wq_bf", [128, 8, DM], BF16)
        wkv_bf = asb("wkv_bf", [128, 8, 2 * DM], BF16)
        x_g = asb("x_g", [128, 2, 4, DM], F32)
        xn_tm = asb("xn_tm", [128, 4, DM], BF16)
        xn_fm = asb("xn_fm", [128, 2, 4, 8, 128], BF16)
        q_stage = asb("q_stage", [128, 2, 8, 512], BF16)
        k_stage = asb("k_stage", [128, 2, 8, 512], BF16)
        v_stage = asb("v_stage", [128, 2, 4, 16 * 65], BF16)
        vtmp = asb("vtmp", [128, 1024], BF16)

        B_wq, B_wkv = [Buf() for _ in range(8)], [Buf() for _ in range(8)]
        B_xg = [[Buf() for _ in range(4)] for _ in range(2)]
        B_xntm = [Buf() for _ in range(4)]
        B_xnfm = [[Buf() for _ in range(4)] for _ in range(2)]
        B_qs = [[Buf() for _ in range(8)] for _ in range(2)]
        B_ks = [[Buf() for _ in range(8)] for _ in range(2)]
        B_vs = [[Buf() for _ in range(4)] for _ in range(2)]
        B_vtmp = Buf()

        for dc in range(8):
            p.emit("gpsimd", "dma_start", wq_bf[:, dc, :],
                   wq[dc * 128:(dc + 1) * 128, :], writes=[B_wq[dc]],
                   dma="wqkv", declare_only=True)
        for dc in range(8):
            p.emit("gpsimd", "dma_start", wkv_bf[:, dc, :],
                   wkv[dc * 128:(dc + 1) * 128, :], writes=[B_wkv[dc]],
                   dma="wqkv", declare_only=True)
        for dc in range(8):
            B_wq[dc].ready = [("D_wqkv", p.cnt["D_wqkv"])]
            B_wkv[dc].ready = [("D_wqkv", p.cnt["D_wqkv"])]

        B_vones = Buf()
        for b in range(2):
            for h in range(16):
                p.emit("vector", "memset",
                       v_stage[:, b, :, h * 65 + 64:h * 65 + 65], 1.0,
                       writes=[B_vones], inc=(b == 1 and h == 15), first=False)

        def emit_lntr(tg):
            bu = tg % 2
            p.emit("sync", "dma_start", x_g[:, bu, :, :],
                   x_in[tg * 512:(tg + 1) * 512, :].rearrange("(t p) d -> p t d", p=128),
                   writes=B_xg[bu], dma="xg%d" % bu, declare_only=True)
            for tt in range(4):
                B_xg[bu][tt].ready = [("D_xg%d" % bu, p.cnt["D_xg%d" % bu])]
                B_xg[bu][tt].free = []
            for tt in range(4):
                layer_norm(x_g[:, bu, tt, :], xn_tm[:, tt, :], B_xg[bu][tt],
                           B_xntm[tt], tt)
                transpose_tile(xn_tm[:, tt, :], xn_fm[:, bu, tt, :, :],
                               B_xntm[tt], B_xnfm[bu][tt])

        def emit_proj(tg):
            bu = tg % 2
            for kind in range(2):
                wbf, wB = (wq_bf, B_wq) if kind == 0 else (wkv_bf, B_wkv)
                stage, sB = (q_stage, B_qs) if kind == 0 else (k_stage, B_ks)
                for fc in range(8):
                    ps = P0 if fc % 2 == 0 else P1
                    pB = B_P[id(ps)]
                    rd_all = list(wB) + [B_xnfm[bu][t] for t in range(4)]
                    for dc in range(8):
                        first, last = dc == 0, dc == 7
                        rd = rd_all if last else [wB[dc]] + [B_xnfm[bu][t] for t in range(4)]
                        p.emit("tensor", "matmul", ps[:, 0:512],
                               wbf[:, dc, fc * 128:(fc + 1) * 128],
                               xn_fm[:, bu, :, dc, :],
                               start=first, stop=last,
                               reads=rd, writes=[pB],
                               inc=last, declare_only=first and not last, first=first)
                    if kind == 0:
                        p.emit("scalar", "mul", stage[:, bu, fc, :], ps[:, 0:512],
                               DH ** -0.5, reads=[pB], writes=[sB[bu][fc]])
                    else:
                        p.emit("scalar", "copy", stage[:, bu, fc, :], ps[:, 0:512],
                               reads=[pB], writes=[sB[bu][fc]])
            for tt in range(4):
                for half in range(2):
                    hB = B_P4h[half]
                    rd_all = list(B_wkv) + [B_xnfm[bu][tt]]
                    for dc in range(8):
                        first, last = dc == 0, dc == 7
                        p.emit("tensor", "matmul", P4[:, half * 512:(half + 1) * 512],
                               xn_fm[:, bu, tt, dc, :],
                               wkv_bf[:, dc, DM + half * 512:DM + (half + 1) * 512],
                               start=first, stop=last,
                               reads=rd_all if last else [B_wkv[dc], B_xnfm[bu][tt]],
                               writes=[hB],
                               inc=last, declare_only=first and not last, first=first)
                    p.emit("scalar", "copy", vtmp[:, half * 512:(half + 1) * 512],
                           P4[:, half * 512:(half + 1) * 512],
                           reads=[hB], writes=[B_vtmp], first=(half == 0),
                           declare_only=(half == 0))
                p.emit("vector", "tensor_copy",
                       v_stage[:, bu, tt, :].rearrange("p (h c) -> p h c", c=65)[:, :, 0:64],
                       vtmp[:].rearrange("p (h c) -> p h c", c=64),
                       reads=[B_vtmp], writes=[B_vs[bu][tt]])
            p.emit("scalar", "dma_start", q_hbm[tg], q_stage[:, bu, :, :],
                   reads=[B_qs[bu][fc] for fc in range(8)], dma="qst%d" % bu)
            p.emit("scalar", "dma_start",
                   k_hbm[:, :, tg * 512:(tg + 1) * 512].rearrange("fc p c -> p fc c"),
                   k_stage[:, bu, :, :],
                   reads=[B_ks[bu][fc] for fc in range(8)], dma="kst%d" % bu)
            p.emit("scalar", "dma_start",
                   v_hbm[tg * 4:(tg + 1) * 4].rearrange("kt p c -> p kt c"),
                   v_stage[:, bu, :, :],
                   reads=B_vs[bu] + [B_vones], dma="vst%d" % bu)

        for tg in range(NG):
            emit_lntr(tg)
            if tg >= 1:
                emit_proj(tg - 1)
        emit_proj(NG - 1)

        p.barrier("A->B")

    with ExitStack() as bctx:
        bsb = lambda *a: bctx.enter_context(nc.sbuf_tensor(*a))
        wo_bf = bsb("wo_bf", [128, 8, DM], BF16)
        v_sb = bsb("v_sb", [128, NT, 16 * 65], BF16)
        k_buf = bsb("k_buf", [128, 2, N], BF16)
        q_buf = bsb("q_buf", [128, 2, 512], BF16)
        pT_sb = bsb("pT_sb", [128, 4096], BF16)
        attn_fm = bsb("attn_fm", [128, 8, 512], BF16)
        attn_f32 = bsb("attn_f32", [128, 512], F32)
        tmp_bf = bsb("tmp_bf", [128, 512], BF16)
        xg2 = bsb("xg2", [128, 4, DM], F32)
        x2_sb = bsb("x2_sb", [128, 4, DM], F32)
        xn2_tm = bsb("xn2_tm", [128, 4, DM], BF16)
        xn2_fm = bsb("xn2_fm", [128, 4, 8, 128], BF16)

        B_wo = [Buf() for _ in range(8)]
        B_vc = [Buf() for _ in range(4)]
        B_kb = [Buf(), Buf()]
        B_qb = [Buf(), Buf()]
        B_pT = [Buf() for _ in range(8)]
        B_afm = [[Buf(), Buf()] for _ in range(8)]
        B_af32 = Buf()
        B_tmpbf = Buf()
        B_xg2 = [Buf() for _ in range(4)]
        B_x2 = [Buf() for _ in range(4)]
        B_xn2tm = [Buf() for _ in range(4)]
        B_xn2fm = [[Buf() for _ in range(4)] for _ in range(8)]

        for dc in range(8):
            p.emit("gpsimd", "dma_start", wo_bf[:, dc, :],
                   wo[dc * 128:(dc + 1) * 128, :], writes=[B_wo[dc]],
                   dma="wo", declare_only=True)
        for dc in range(8):
            B_wo[dc].ready = [("D_wo", p.cnt["D_wo"])]
        def load_kq(qc, fc):
            par = fc % 2
            nkb = 4 * qc + 4
            p.emit("sync", "dma_start", k_buf[:, par, 0:nkb * 128],
                   k_hbm[fc, :, 0:nkb * 128], writes=[B_kb[par]],
                   dma="kq%d" % par, declare_only=True)
            p.emit("sync", "dma_start", q_buf[:, par, :],
                   q_hbm[qc, :, fc, :], writes=[B_qb[par]],
                   dma="kq%d" % par, declare_only=True)
            tot = [("D_kq%d" % par, p.cnt["D_kq%d" % par])]
            for b in (B_kb[par], B_qb[par]):
                b.ready = list(tot)
                b.free = []

        load_kq(0, 0)
        VCH = max(1, NT // 4)
        for vc in range(4):
            lo, hi = vc * VCH, min((vc + 1) * VCH, NT)
            if lo >= hi:
                B_vc[vc].ready = list(B_vc[vc - 1].ready)
                continue
            p.emit("sync", "dma_start", v_sb[:, lo:hi, :],
                   v_hbm[lo:hi].rearrange("kt p c -> p kt c"),
                   writes=[B_vc[vc]], dma="vl%d" % vc)
        zero_sb = bsb("zero_sb", [128, DM], BF16)
        B_zero = Buf()
        B_zx = Buf()
        p.emit("vector", "memset", zero_sb[:], 0.0, writes=[B_zero])
        nzt = NE * CAP // 128 + 1
        for zi in range(nzt):
            p.emit("scalar", "dma_start", xs_hbm[zi * 128:(zi + 1) * 128, :],
                   zero_sb[:], reads=[B_zero], dma="zx", declare_only=True)
        p.emit("scalar", "dma_start", ys_hbm[NE * CAP:NE * CAP + 128, :],
               zero_sb[:], reads=[B_zero], dma="zx", declare_only=True)
        B_zx.ready = [("D_zx", p.cnt["D_zx"])]
        B_zx_holder[0] = B_zx


        pend_epi = [None]
        pend_osw = [None]
        for qc in range(NG):
            nkb = 4 * qc + 4
            p.emit("gpsimd", "dma_start", xg2[:],
                   x_in[qc * 512:(qc + 1) * 512, :].rearrange("(t p) d -> p t d", p=128),
                   writes=B_xg2, dma="xb0", declare_only=True)
            for tt in range(4):
                B_xg2[tt].ready = [("D_xb0", p.cnt["D_xb0"])]
                B_xg2[tt].free = []
            for fc in range(8):
                par = fc % 2
                nfc, nqc = fc + 1, qc
                if nfc == 8:
                    nfc, nqc = 0, qc + 1
                if nqc < NG:
                    load_kq(nqc, nfc)
                SPS = (P0, P1)
                nkp = nkb // 2

                def emit_s(h2, kb, j):
                    ps = SPS[h2]
                    p.emit("tensor", "matmul", ps[:, j * 512:(j + 1) * 512],
                           k_buf[h2 * 64:h2 * 64 + 64, par, kb * 128:(kb + 1) * 128],
                           q_buf[h2 * 64:h2 * 64 + 64, par, :],
                           start=True, stop=True,
                           reads=[B_kb[par], B_qb[par]], writes=[B_P[id(ps)]],
                           inc=(j == 1), declare_only=(j == 0), first=(j == 0))

                def emit_exp_both(pp):
                    base = (pp % 2) * 4
                    p.emit("scalar", "activation",
                           pT_sb[:, base * 512:(base + 4) * 512], PS01[:], AF.Exp,
                           reads=[B_P[id(P0)], B_P[id(P1)]],
                           writes=[B_pT[base], B_pT[base + 1],
                                   B_pT[base + 2], B_pT[base + 3]])
                    for h2 in range(2):
                        for j in range(2):
                            kb = 2 * pp + j
                            d = kb - 4 * qc
                            if d >= 0:
                                sl = (pp % 2) * 4 + 2 * h2 + j
                                p.emit("vector", "tensor_mul",
                                       pT_sb[:, sl * 512:(sl + 1) * 512],
                                       pT_sb[:, sl * 512:(sl + 1) * 512],
                                       maskd[:, d, :],
                                       reads=[B_pT[sl], B_consts],
                                       writes=[B_pT[sl]], first=False)

                pv_ps_l = (P2[0:65, :], P4[0:65, 0:512])
                pv_B_l = (B_P[id(P2)], B_P4h[0])

                def emit_pv(h2, kb):
                    sl = (kb // 2 % 2) * 4 + 2 * h2 + kb % 2
                    first, last = kb == 0, kb == nkb - 1
                    odd = kb % 2 == 1
                    rd = [B_vc[min(3, kb // VCH)], B_pT[sl]]
                    if odd:
                        rd.append(B_pT[sl - 1])
                    p.emit("tensor", "matmul", pv_ps_l[h2],
                           v_sb[:, kb, (2 * fc + h2) * 65:(2 * fc + h2) * 65 + 65],
                           pT_sb[:, sl * 512:(sl + 1) * 512],
                           start=first, stop=last,
                           reads=rd, writes=[pv_B_l[h2]],
                           inc=odd, declare_only=not last, first=first)

                for pp in range(nkp):
                    for j in range(2):
                        for h2 in range(2):
                            emit_s(h2, 2 * pp + j, j)
                    emit_exp_both(pp)
                    if pp == 0 and fc == 2 and pend_osw[0] is not None:
                        pend_osw[0]()
                        pend_osw[0] = None
                    if pp == 1 and pend_epi[0] is not None:
                        pend_epi[0]()
                        pend_epi[0] = None
                    if pp >= 1:
                        for h2 in range(2):
                            emit_pv(h2, 2 * (pp - 1))
                            emit_pv(h2, 2 * (pp - 1) + 1)
                for h2 in range(2):
                    emit_pv(h2, nkb - 2)
                    emit_pv(h2, nkb - 1)

                for h2 in range(2):
                    lc = slice(h2 * 512, h2 * 512 + 512)
                    p.emit("vector", "tensor_scalar_add", lr_sb[64:65, lc],
                           pv_ps_l[h2][64:65, :], EPS_ATT,
                           reads=[pv_B_l[h2]], writes=[B_l[h2]])
                    p.emit("scalar", "activation", lr_sb[64:65, lc],
                           lr_sb[64:65, lc], AF.Ln,
                           reads=[B_l[h2]], writes=[B_l[h2]], first=False)
                    p.emit("scalar", "activation", r_bf[64:65, lc],
                           lr_sb[64:65, lc], AF.Exp, scale=-1.0,
                           reads=[B_l[h2]], writes=[B_r[h2]])

                def make_epilogue(fc=fc):
                    def epi():
                        for h2 in range(2):
                            lc = slice(h2 * 512, h2 * 512 + 512)
                            p.emit("tensor", "matmul", P4[0:64, 512:1024],
                                   ones_bf[64:65, :], r_bf[64:65, lc],
                                   start=True, stop=True,
                                   reads=[B_r[h2], B_consts], writes=[B_P4h[1]])
                            p.emit("vector", "tensor_copy", attn_f32[0:64, :],
                                   pv_ps_l[h2][0:64, :],
                                   reads=[pv_B_l[h2]], writes=[B_af32])
                            if h2 == 0:
                                p.emit("vector", "tensor_mul",
                                       attn_fm[0:64, fc, :],
                                       attn_f32[0:64, :], P4[0:64, 512:1024],
                                       reads=[B_af32, B_P4h[1]],
                                       writes=[B_afm[fc][0]])
                            else:
                                p.emit("vector", "tensor_mul", tmp_bf[0:64, :],
                                       attn_f32[0:64, :], P4[0:64, 512:1024],
                                       reads=[B_af32, B_P4h[1]], writes=[B_tmpbf])
                                p.emit("tensor", "matmul", P4[64:128, 0:512],
                                       ident[0:64, 0:64], tmp_bf[0:64, :],
                                       start=True, stop=True, tile_position=(0, 64),
                                       reads=[B_tmpbf, B_consts], writes=[B_P4h[0]])
                                p.emit("vector", "tensor_copy",
                                       attn_fm[64:128, fc, :], P4[64:128, 0:512],
                                       reads=[B_P4h[0]], writes=[B_afm[fc][1]])
                    return epi
                pend_epi[0] = make_epilogue()
            if pend_epi[0] is not None:
                pend_epi[0]()
                pend_epi[0] = None
            for qt in range(4):
                for half in range(2):
                    hB = B_P4h[half]
                    rd_all = list(B_wo) + [B_afm[f][j] for f in range(8) for j in range(2)]
                    for fc in range(8):
                        first, last = fc == 0, fc == 7
                        p.emit("tensor", "matmul", P4[:, half * 512:(half + 1) * 512],
                               attn_fm[:, fc, qt * 128:(qt + 1) * 128],
                               wo_bf[:, fc, half * 512:(half + 1) * 512],
                               start=first, stop=last,
                               reads=rd_all if last else
                               [B_wo[fc], B_afm[fc][0], B_afm[fc][1]],
                               writes=[hB],
                               inc=last, declare_only=first and not last, first=first)
                    sl = slice(half * 512, (half + 1) * 512)
                    p.emit("vector", "tensor_add", x2_sb[:, qt, sl],
                           P4[:, sl], bo_sb[:, sl],
                           reads=[hB, B_consts], writes=[B_x2[qt]])
                    p.emit("vector", "tensor_add", x2_sb[:, qt, sl],
                           x2_sb[:, qt, sl], xg2[:, qt, sl],
                           reads=[B_xg2[qt], B_x2[qt]],
                           writes=[B_x2[qt]], first=False)
                layer_norm(x2_sb[:, qt, :], xn2_tm[:, qt, :], B_x2[qt],
                           B_xn2tm[qt], qt)
            def make_osweep3(qc=qc):
                def osw3():
                    for qt in range(4):
                        transpose_tile(xn2_tm[:, qt, :], xn2_fm[:, qt, :, :],
                                       B_xn2tm[qt], B_xn2fm[0][qt])
                        rt_B = B_P4h[1]
                        rt_rd = [B_xn2fm[0][qt], B_wr]
                        for dc in range(8):
                            first, last = dc == 0, dc == 7
                            p.emit("tensor", "matmul", P4[:, 512:520],
                                   xn2_fm[:, qt, dc, :],
                                   wr_bf[:, dc, :],
                                   start=first, stop=last,
                                   reads=rt_rd if last else [B_xn2fm[0][qt], B_wr],
                                   writes=[rt_B],
                                   inc=last, declare_only=first and not last,
                                   first=first)
                        p.emit("vector", "tensor_add", lg_sb[:, qt, :],
                               P4[:, 512:520], br_sb[:],
                               reads=[rt_B, B_consts], writes=[B_rt["lg"][qt]])
                        p.emit("vector", "reduce_max", lmax_sb[:, qt:qt + 1],
                               lg_sb[:, qt, :], AX.X,
                               reads=[B_rt["lg"][qt]], writes=[B_rt["lmax"][qt]])
                        p.emit("vector", "tensor_scalar", sub_sb[:, qt, :],
                               lg_sb[:, qt, :], lmax_sb[:, qt:qt + 1], None,
                               ALU.subtract,
                               reads=[B_rt["lg"][qt], B_rt["lmax"][qt]],
                               writes=[B_rt["sub"][qt]])
                        p.emit("scalar", "sign", sgn_sb[:, qt, :], sub_sb[:, qt, :],
                               reads=[B_rt["sub"][qt]], writes=[B_rt["sgn"][qt]])
                        p.emit("scalar", "activation", pe_sb[:, qt, :],
                               sub_sb[:, qt, :], AF.Exp,
                               reads=[B_rt["sub"][qt]], writes=[B_rt["pe"][qt]])
                        p.emit("vector", "reduce_sum", den_sb[:, qt:qt + 1],
                               pe_sb[:, qt, :], AX.X,
                               reads=[B_rt["pe"][qt]], writes=[B_rt["den"][qt]])
                        ti = qc * 4 + qt
                        p.emit("vector", "reciprocal", gates_sb[:, ti:ti + 1],
                               den_sb[:, qt:qt + 1],
                               reads=[B_rt["den"][qt]], writes=[B_gate[ti]])
                        p.emit("vector", "tensor_scalar_add", mask_bf[:, qt, :],
                               sgn_sb[:, qt, :], 1.0,
                               reads=[B_rt["sgn"][qt]], writes=[B_maskbf[qt]])
                        s = tr_slot[0] % 8
                        tr_slot[0] += 1
                        p.emit("tensor", "transpose", P3[0:8, s * 128:(s + 1) * 128],
                               mask_bf[:, qt, :], ident[:],
                               reads=[B_maskbf[qt], B_consts], writes=[B_P3])
                        p.emit("vector", "tensor_copy",
                               maskT_sb[0:8, ti * 128:(ti + 1) * 128],
                               P3[0:8, s * 128:s * 128 + 128],
                               reads=[B_P3], writes=[B_maskT[ti]])
                    p.emit("scalar", "dma_start",
                           xn2tm_hbm[qc * 512:(qc + 1) * 512, :].rearrange(
                               "(t p) d -> p t d", p=128),
                           xn2_tm[:],
                           reads=[B_xn2tm[qt] for qt in range(4)], dma="xn2st")
                return osw3
            pend_osw[0] = make_osweep3()
            p.emit("scalar", "dma_start",
                   x2_hbm[qc * 512:(qc + 1) * 512, :].rearrange("(t p) d -> p t d", p=128),
                   x2_sb[:], reads=B_x2, dma="x2st")

        if pend_osw[0] is not None:
            pend_osw[0]()
            pend_osw[0] = None
        p.barrier("B->R")

    with ExitStack() as rctx:
        rsb = lambda *a, **k: rctx.enter_context(nc.sbuf_tensor(*a, **k))
        cums = rsb("cums", [8, N], F32)
        ones8_sb = rsb("ones8_sb", [8, 1], F32)
        basem1_sb = rsb("basem1_sb", [8, 1], F32)
        slots_f = rsb("slots_f", [128, NT], F32)
        xtile = rsb("xtile", [128, 2, DM], BF16)
        B_cums = Buf()
        B_slotf = Buf()
        B_small = Buf()
        B_xt = [Buf(), Buf()]
        B_zx = B_zx_holder[0]

        p.emit("sync", "dma_start", ones8_sb[:], ones8_in, writes=[B_small],
               dma="r8", declare_only=True)
        p.emit("sync", "dma_start", basem1_sb[:], basem1_in, writes=[B_small],
               dma="r8", declare_only=True)
        B_small.ready = [("D_r8", p.cnt["D_r8"])]

        p.emit("vector", "tensor_tensor_scan", cums[:], maskT_sb[:], maskT_sb[:],
               0.0, ALU.add, ALU.bypass,
               reads=B_maskT, writes=[B_cums])
        p.emit("vector", "tensor_scalar_add", cums[:], cums[:],
               basem1_sb[:, 0:1], reads=[B_cums, B_small], writes=[B_cums],
               first=False)
        p.emit("vector", "tensor_mul", cums[:], cums[:], maskT_sb[:],
               reads=[B_cums] + B_maskT, writes=[B_cums], first=False)
        for ti in range(NT):
            p.emit("tensor", "matmul", P2[:, ti:ti + 1],
                   cums[:, ti * 128:(ti + 1) * 128], ones8_sb[:],
                   start=True, stop=True,
                   reads=[B_cums, B_small], writes=[B_P[id(P2)]],
                   inc=(ti == NT - 1), declare_only=(ti < NT - 1),
                   first=(ti == 0))
        p.emit("vector", "tensor_scalar_min", slots_f[:], P2[:, 0:NT],
               float(NE * CAP), reads=[B_P[id(P2)]], writes=[B_slotf])
        p.emit("vector", "tensor_copy", slot_sb[:], slots_f[:],
               reads=[B_slotf], writes=[B_slot])
        for ti in range(NT):
            bu = ti % 2
            p.emit("sync", "dma_start", xtile[:, bu, :],
                   xn2tm_hbm[ti * 128:(ti + 1) * 128, :],
                   writes=[B_xt[bu]], dma="xt%d" % bu)
            p.emit("gpsimd", "indirect_dma_start", xs_hbm,
                   IndirectOffsetOnAxis(slot_sb[:, ti:ti + 1], 0),
                   xtile[:, bu, :], None,
                   reads=[B_xt[bu], B_slot, B_zx], dma="sc%d" % bu)

        p.barrier("R->C")

    ab_ctx.close()

    CGRP = [(o, min(512, CAP - o)) for o in range(0, CAP, 512)]
    NTE = CAP // 128
    with ExitStack() as cctx:
        csb = lambda *a: cctx.enter_context(nc.sbuf_tensor(*a))
        w1_bf = csb("w1_bf", [128, 2, 8, HFF], BF16)
        w2_bf = csb("w2_bf", [128, 16, DM], BF16)
        xsf_sb = csb("xsf_sb", [128, 3, 8, 512], BF16)
        h_buf = csb("h_buf", [128, 16, 512], BF16)
        y_tmp = csb("y_tmp", [128, DM], F32)
        ys_bf = csb("ys_bf", [128, 2, DM], BF16)
        b1_sb = csb("b1_sb", [128, 16], F32)
        b2_sb = csb("b2_sb", [128, DM], F32)

        B_w1 = [[Buf() for _ in range(8)] for _ in range(2)]
        B_w2 = [Buf() for _ in range(16)]
        B_xsf = [Buf(), Buf(), Buf()]
        B_h = [Buf() for _ in range(16)]
        B_yt = Buf()
        B_ysb = [Buf(), Buf()]
        B_b1 = Buf()
        B_b2 = Buf()

        def load_expert(e):
            wp = e % 2
            k1, k2 = "w1e%d" % wp, "w2e%d" % wp
            for dc in range(8):
                p.emit("gpsimd", "dma_start", w1_bf[:, wp, dc, :],
                       w1[e, dc * 128:(dc + 1) * 128, :], writes=[B_w1[wp][dc]],
                       dma=k1, declare_only=True)
            p.emit("gpsimd", "dma_start", b1_sb[:], b1_l[e], writes=[B_b1],
                   dma=k1, declare_only=True)
            tot1 = [("D_" + k1, p.cnt["D_" + k1])]
            for dc in range(8):
                B_w1[wp][dc].ready = list(tot1)
            B_b1.ready = list(tot1)
            for hc in range(16):
                p.emit("gpsimd", "dma_start", w2_bf[:, hc, :],
                       w2[e, hc * 128:(hc + 1) * 128, :], writes=[B_w2[hc]],
                       dma=k2, declare_only=True)
            p.emit("gpsimd", "dma_start", b2_sb[:], b2_b[e], writes=[B_b2],
                   dma=k2, declare_only=True)
            tot2 = [("D_" + k2, p.cnt["D_" + k2])]
            for hc in range(16):
                B_w2[hc].ready = list(tot2)
            B_b2.ready = list(tot2)

        xs_count = [0]
        ys_count = [0]

        def load_xs(e, gi):
            off, ln = CGRP[gi]
            bu = xs_count[0] % 3
            xs_count[0] += 1
            for dc in range(8):
                p.emit("sync", "dma_start_transpose",
                       xsf_sb[:, bu, dc, 0:ln],
                       xs_hbm[e * CAP + off:e * CAP + off + ln,
                              dc * 128:(dc + 1) * 128],
                       writes=[B_xsf[bu]], dma="xf%d" % bu, declare_only=True)
            B_xsf[bu].ready = [("D_xf%d" % bu, p.cnt["D_xf%d" % bu])]
            B_xsf[bu].free = []

        NGRP = len(CGRP)
        load_expert(0)
        load_xs(0, 0)
        if NGRP > 1:
            load_xs(0, 1)
        else:
            load_xs(1, 0)
        xs_use = [0]
        for e in range(NE):
            for gi, (off, ln) in enumerate(CGRP):
                bu = xs_use[0] % 3
                xs_use[0] += 1
                g2 = e * NGRP + gi + 2
                if g2 < NE * NGRP:
                    load_xs(g2 // NGRP, g2 % NGRP)
                wp = e % 2
                for hc in range(16):
                    ps = P0 if hc % 2 == 0 else P1
                    pB = B_P[id(ps)]
                    h_rd = list(B_w1[wp]) + [B_xsf[bu]]
                    for dc in range(8):
                        first, last = dc == 0, dc == 7
                        p.emit("tensor", "matmul", ps[:, 0:ln],
                               w1_bf[:, wp, dc, hc * 128:(hc + 1) * 128],
                               xsf_sb[:, bu, dc, 0:ln],
                               start=first, stop=last,
                               reads=h_rd if last else [B_w1[wp][dc], B_xsf[bu]],
                               writes=[pB],
                               inc=last, declare_only=first and not last,
                               first=first)
                    p.emit("scalar", "activation", h_buf[:, hc, 0:ln],
                           ps[:, 0:ln], AF.Relu, bias=b1_sb[:, hc:hc + 1],
                           reads=[pB, B_b1], writes=[B_h[hc]])
                for qt in range(ln // 128):
                    bu2 = ys_count[0] % 2
                    ys_count[0] += 1
                    for half in range(2):
                        hB = B_P4h[half]
                        y_rd = list(B_h) + list(B_w2)
                        for hc in range(16):
                            first, last = hc == 0, hc == 15
                            p.emit("tensor", "matmul",
                                   P4[:, half * 512:(half + 1) * 512],
                                   h_buf[:, hc, qt * 128:(qt + 1) * 128],
                                   w2_bf[:, hc, half * 512:(half + 1) * 512],
                                   start=first, stop=last,
                                   reads=y_rd if last else [B_h[hc], B_w2[hc]],
                                   writes=[hB],
                                   inc=last, declare_only=first and not last,
                                   first=first)
                        sl = slice(half * 512, (half + 1) * 512)
                        p.emit("vector", "tensor_add", y_tmp[:, sl],
                               P4[:, sl], b2_sb[:, sl],
                               reads=[hB, B_b2], writes=[B_yt],
                               first=(half == 0))
                    p.emit("scalar", "copy", ys_bf[:, bu2, :], y_tmp[:],
                           reads=[B_yt], writes=[B_ysb[bu2]])
                    row = e * CAP + off + qt * 128
                    p.emit("scalar", "dma_start",
                           ys_hbm[row:row + 128, :],
                           ys_bf[:, bu2, :], reads=[B_ysb[bu2]], dma="yso%d" % bu2)
            if e + 1 < NE:
                load_expert(e + 1)

        ys_done = [(s, v) for s, v in p.cnt.items() if s.startswith("D_yso")]

        with ExitStack() as fctx:
            fsb = lambda *a: fctx.enter_context(nc.sbuf_tensor(*a))
            x2g = fsb("x2g", [128, 4, DM], F32)
            y_stage = fsb("y_stage", [128, 4, DM], F32)
            yg_sb = fsb("yg_sb", [128, 2, DM], BF16)
            ygs_sb = fsb("ygs_sb", [128, 2, DM], F32)
            B_x2g = Buf()
            B_ys = Buf()
            B_yg = [Buf(), Buf()]
            B_ygs = [Buf(), Buf()]
            for qc in range(NG):
                p.emit("sync", "dma_start", x2g[:],
                       x2_hbm[qc * 512:(qc + 1) * 512, :].rearrange(
                           "(t p) d -> p t d", p=128),
                       writes=[B_x2g], dma="x2g0")
                for qt in range(4):
                    ti = qc * 4 + qt
                    tb = ti % 2
                    p.emit("gpsimd", "indirect_dma_start", yg_sb[:, tb, :], None,
                           ys_hbm, IndirectOffsetOnAxis(slot_sb[:, ti:ti + 1], 0),
                           reads=[B_slot], writes=[B_yg[tb]], dma="gb%d" % tb,
                           extra=ys_done)
                    p.emit("scalar", "activation", ygs_sb[:, tb, :],
                           yg_sb[:, tb, :], AF.Copy,
                           scale=gates_sb[:, ti:ti + 1],
                           reads=[B_yg[tb], B_gate[ti]], writes=[B_ygs[tb]])
                    p.emit("vector", "tensor_add", y_stage[:, qt, :],
                           x2g[:, qt, :], ygs_sb[:, tb, :],
                           reads=[B_x2g, B_ygs[tb]], writes=[B_ys],
                           first=(qt == 0), declare_only=(qt < 3))
                p.emit("scalar", "dma_start",
                       y_out[qc * 512:(qc + 1) * 512, :].rearrange(
                           "(t p) d -> p t d", p=128),
                       y_stage[:], reads=[B_ys], dma="yst0")

    final_waits = [(s, v) for s, v in p.cnt.items() if s.startswith("D_")]
    p.wait_only("sync", final_waits)
    p.wait_only("scalar", final_waits)

    with ctx:
        sems = {}
        with ExitStack() as sctx:
            for name in p.cnt:
                sems[name] = sctx.enter_context(nc.semaphore(name))
            with nc.Block() as block:
                p.realize(block, sems)
    return nc


_NC_CACHE = {}


def _get_nc(debug=False):
    key = bool(debug)
    if key not in _NC_CACHE:
        _NC_CACHE[key] = build_program(debug=debug)
    return _NC_CACHE[key]


def _prep_inputs(inputs):
    g = {k: np.asarray(v) for k, v in inputs.items()}
    x = np.ascontiguousarray(g["L0_attn_x"], dtype=np.float32)
    wq = np.ascontiguousarray(g["L0_attn_Wq"], dtype=np.float32)
    wkv = np.ascontiguousarray(g["L0_attn_Wkv"], dtype=np.float32)
    wo = np.ascontiguousarray(g["L0_attn_Wo"], dtype=np.float32)
    bo = np.ascontiguousarray(g["L0_attn_bo"], dtype=np.float32)
    wr = np.ascontiguousarray(g["L0_ffn_Wr"], dtype=np.float32)
    br = np.ascontiguousarray(g["L0_ffn_br"], dtype=np.float32)
    w1 = np.ascontiguousarray(g["L0_ffn_W1"], dtype=np.float32)
    b1 = np.ascontiguousarray(g["L0_ffn_b1"], dtype=np.float32)
    w2 = np.ascontiguousarray(g["L0_ffn_W2"], dtype=np.float32)
    b2 = np.ascontiguousarray(g["L0_ffn_b2"], dtype=np.float32)

    bo_b = np.ascontiguousarray(np.broadcast_to(bo[None, :], (128, DM)))
    br_b = np.ascontiguousarray(np.broadcast_to(br[None, :], (128, NE)))
    b1_l = np.ascontiguousarray(b1.reshape(NE, 16, 128).transpose(0, 2, 1))
    b2_b = np.ascontiguousarray(np.broadcast_to(b2[:, None, :], (NE, 128, DM)))

    kk = np.arange(128)[:, None]
    qq = np.arange(512)[None, :]
    maskd = np.stack([(qq - 128 * d >= kk) for d in range(4)], axis=1)
    maskd = maskd.astype(ml_dtypes.bfloat16)
    ident = np.eye(128, dtype=ml_dtypes.bfloat16)
    ones_b = np.ones((128, 64), dtype=np.float32)
    ones8 = np.ones((8, 1), dtype=np.float32)
    cap = max(256, ((N * 5 // 16) + 127) // 128 * 128)
    basem1 = (np.arange(NE, dtype=np.float32)[:, None] * cap) - 1.0

    shared = dict(wq=wq, wkv=wkv, wo=wo, bo_b=bo_b, wr=wr, br_b=br_b,
                  w1=w1, b1_l=b1_l, w2=w2, b2_b=b2_b,
                  maskd=np.ascontiguousarray(maskd), ident=ident, ones_b=ones_b,
                  ones8=ones8, basem1=basem1)
    in_maps = []
    for b in range(8):
        m = dict(shared)
        m["x_in"] = np.ascontiguousarray(x[b])
        in_maps.append(m)
    return in_maps


def kernel(**inputs):
    in_maps = _prep_inputs(inputs)
    nc = _get_nc(debug=False)
    res = run_bass_kernel_spmd(nc, in_maps, core_ids=list(range(8)))
    out = np.stack([np.asarray(r["y_out"], dtype=np.float32) for r in res.results])
    return out


# version 27
# speedup vs baseline: 1.0225x; 1.0225x over previous
import numpy as np
import ml_dtypes
import concourse.bass as bass
import concourse.mybir as mybir
from concourse.bass_utils import run_bass_kernel_spmd
from concourse.bass_core_rust import IndirectOffsetOnAxis
from contextlib import ExitStack

F32 = mybir.dt.float32
BF16 = mybir.dt.bfloat16
AF = mybir.ActivationFunctionType
ALU = mybir.AluOpType
AX = mybir.AxisListType

N, DM, NHEAD, DH, NE, HFF = 4096, 1024, 16, 64, 8, 2048
NT, NG = 32, 8
EPS_LN = 1e-5
EPS_ATT = 1e-8

ENGS = ("sync", "scalar", "vector", "tensor", "gpsimd")


class Buf:
    __slots__ = ("ready", "free")

    def __init__(self):
        self.ready = []
        self.free = []


class Prog:

    def __init__(self):
        self.recs = {e: [] for e in ENGS}
        self.cnt = {}
        for e in ENGS:
            self.cnt["e_" + e] = 0
        self.hwm = {e: {} for e in ENGS}

    def _coalesce(self, eng, waits):
        best = {}
        for s, v in waits:
            if v > best.get(s, 0):
                best[s] = v
        out = []
        for s, v in best.items():
            assert v <= self.cnt[s], f"wait on future value of {s}: {v} > {self.cnt[s]}"
            if self.hwm[eng].get(s, 0) < v:
                self.hwm[eng][s] = v
                out.append((s, v))
        return out

    def emit(self, eng, method, *args, reads=(), writes=(), extra=(), dma=None,
             inc=True, declare_only=False, first=True, **kwargs):
        waits = list(extra)
        for b in reads:
            waits += b.ready
        if first:
            for b in writes:
                waits += b.free + b.ready
        waits = self._coalesce(eng, waits)
        if dma is not None:
            sem = "D_" + dma
            if sem not in self.cnt:
                self.cnt[sem] = 0
        else:
            sem = "e_" + eng
        if inc:
            self.cnt[sem] += 16 if dma else 1
            val = self.cnt[sem]
        self.recs[eng].append((waits, method, args, kwargs,
                               sem if inc else None, 16 if dma else 1))
        if inc:
            if not declare_only:
                for b in writes:
                    b.ready = [(sem, val)]
                    b.free = []
            for b in reads:
                b.free.append((sem, val))
        return None

    def wait_only(self, eng, extra):
        waits = self._coalesce(eng, list(extra))
        if waits:
            self.recs[eng].append((waits, None, (), {}, None, 0))

    def barrier(self, tag=""):
        targets = []
        for e in ENGS:
            self.emit(e, "nop", extra=[("e_" + e, self.cnt["e_" + e])])
            targets.append(("e_" + e, self.cnt["e_" + e]))
        for s, v in self.cnt.items():
            if s.startswith("D_"):
                targets.append((s, v))
        for e in ENGS:
            self.wait_only(e, targets)

    def realize(self, block, sems):
        for eng in ENGS:
            recs = self.recs[eng]
            if not recs:
                continue

            def body(E, recs=recs):
                for waits, method, args, kwargs, sem, n in recs:
                    for s, v in waits:
                        E.wait_ge(sems[s], v)
                    if method is None:
                        continue
                    inst = getattr(E, method)(*args, **kwargs)
                    if sem is not None:
                        inst.then_inc(sems[sem], n)
            getattr(block, eng)(body)


def build_program(debug=False):
    nc = bass.Bass("TRN2", target_bir_lowering=False)
    dram = {}

    def din(name, shape, dtype=F32):
        dram[name] = nc.dram_tensor(name, shape, dtype, kind="ExternalInput").ap()
        return dram[name]

    def dscr(name, shape, dtype):
        kind = "ExternalOutput" if debug else "Internal"
        dram[name] = nc.dram_tensor(name, shape, dtype, kind=kind).ap()
        return dram[name]

    x_in = din("x_in", [N, DM])
    wq = din("wq", [DM, DM])
    wkv = din("wkv", [DM, 2 * DM])
    wo = din("wo", [DM, DM])
    bo_b = din("bo_b", [128, DM])
    wr = din("wr", [DM, NE])
    br_b = din("br_b", [128, NE])
    w1 = din("w1", [NE, DM, HFF])
    b1_l = din("b1_l", [NE, 128, 16])
    w2 = din("w2", [NE, HFF, DM])
    b2_b = din("b2_b", [NE, 128, DM])
    maskd_in = din("maskd", [128, 4, 512], BF16)
    ident_in = din("ident", [128, 128], BF16)
    ones_in = din("ones_b", [128, 64])
    ones8_in = din("ones8", [8, 1])
    basem1_in = din("basem1", [8, 1])

    y_out = nc.dram_tensor("y_out", [N, DM], F32, kind="ExternalOutput").ap()

    q_hbm = dscr("q_hbm", [NG, 128, 8, 512], BF16)
    k_hbm = dscr("k_hbm", [8, 128, N], BF16)
    v_hbm = dscr("v_hbm", [NT, 128, 16 * 65], BF16)
    x2_hbm = dscr("x2_hbm", [N, DM], F32)
    xn2tm_hbm = dscr("xn2tm_hbm", [N, DM], BF16)
    CAP = max(256, ((N * 5 // 16) + 127) // 128 * 128)
    xs_hbm = dscr("xs_hbm", [NE * CAP + 128, DM], BF16)
    ys_hbm = dscr("ys_hbm", [NE * CAP + 128, DM], BF16)

    p = Prog()
    ctx = ExitStack()
    B_zx_holder = [None]

    sb = lambda *a, **k: ctx.enter_context(nc.sbuf_tensor(*a, **k))
    ident = sb("ident_s", [128, 128], BF16)
    maskd = sb("maskd_s", [128, 4, 512], BF16)
    ones_sb = sb("ones_sb", [128, 64], F32)
    ones_bf = sb("ones_bf", [128, 64], BF16)
    r_bf = sb("r_bf", [128, 1024], BF16)
    bo_sb = sb("bo_sb", [128, DM], F32)
    br_sb = sb("br_sb", [128, NE], F32)
    wr_bf = sb("wr_bf", [128, 8, NE], BF16)
    gates_sb = sb("gates_sb", [128, NT], F32)
    slot_sb = sb("slot_sb", [128, NT], mybir.dt.int32)
    lg_sb = sb("lg_sb", [128, 4, NE], F32)
    lmax_sb = sb("lmax_sb", [128, 4], F32)
    sub_sb = sb("sub_sb", [128, 4, NE], F32)
    sgn_sb = sb("sgn_sb", [128, 4, NE], F32)
    pe_sb = sb("pe_sb", [128, 4, NE], F32)
    den_sb = sb("den_sb", [128, 4], F32)
    gate_sb = sb("gate_sb", [128, 4], F32)

    ab_ctx = ExitStack()
    absb = lambda *a, **k: ab_ctx.enter_context(nc.sbuf_tensor(*a, **k))
    nm_sb = absb("nm_sb", [128, 4], F32)
    var_sb = absb("var_sb", [128, 4], F32)
    rstd_sb = absb("rstd_sb", [128, 4], F32)
    mr_sb = absb("mr_sb", [128, 4], F32)
    sq_sb = absb("sq_sb", [128, 2, DM], F32)
    lr_sb = absb("lr_sb", [128, 1024], F32)
    maskT_sb = absb("maskT_sb", [8, N], BF16)
    mask_bf = absb("mask_bf", [128, 4, NE], BF16)
    B_maskbf = [Buf() for _ in range(4)]

    P0 = ctx.enter_context(nc.psum_tensor("P0", [128, 1024], F32))
    P1 = ctx.enter_context(nc.psum_tensor("P1", [128, 1024], F32))
    P2 = ctx.enter_context(nc.psum_tensor("P2", [128, 512], F32))
    P3 = ctx.enter_context(nc.psum_tensor("P3", [128, 1024], BF16))
    P4 = ctx.enter_context(nc.psum_tensor("P4", [128, 1024], F32))

    B_consts = Buf()
    B_ln = {k: [Buf() for _ in range(4)] for k in ("nm", "var", "rstd", "mr", "sq")}
    B_l = [Buf(), Buf()]
    B_r = [Buf(), Buf()]
    B_rt = {k: [Buf() for _ in range(4)] for k in
            ("lg", "lmax", "sub", "sgn", "pe", "den", "gate")}
    B_gate = [Buf() for _ in range(NT)]
    B_slot = Buf()
    B_maskT = [Buf() for _ in range(NT)]
    B_P = {id(P0): Buf(), id(P1): Buf(), id(P2): Buf(), id(P4): Buf()}
    B_P4h = [Buf(), Buf()]
    B_P3 = Buf()

    p.emit("sync", "dma_start", ident[:], ident_in, writes=[B_consts], dma="cst")
    p.emit("sync", "dma_start", maskd[:], maskd_in, writes=[B_consts], dma="cst")
    p.emit("sync", "dma_start", ones_sb[:], ones_in, writes=[B_consts], dma="cst")
    p.emit("sync", "dma_start", bo_sb[:], bo_b, writes=[B_consts], dma="cst")
    p.emit("sync", "dma_start", br_sb[:], br_b, writes=[B_consts], dma="cst")
    p.emit("vector", "tensor_copy", ones_bf[:], ones_sb[:],
           reads=[B_consts], writes=[B_consts], first=False)
    B_wr = Buf()
    p.emit("gpsimd", "dma_start", wr_bf[:],
           wr.rearrange("(dc p) e -> p dc e", p=128), writes=[B_wr], dma="wrld")

    def layer_norm(x_ap, out_ap, B_x, B_out, sl):
        nm, var, rstd = nm_sb[:, sl:sl + 1], var_sb[:, sl:sl + 1], rstd_sb[:, sl:sl + 1]
        mr = mr_sb[:, sl:sl + 1]
        sq = sq_sb[:, sl % 2, :]
        Bn, Bv, Br_, Bmr, Bsq = (B_ln["nm"][sl], B_ln["var"][sl], B_ln["rstd"][sl],
                                 B_ln["mr"][sl], B_ln["sq"][sl % 2])
        p.emit("vector", "reduce_sum", nm, x_ap, AX.X,
               reads=[B_x], writes=[Bn])
        p.emit("scalar", "activation", sq, x_ap, AF.Square, accum_out=var,
               reads=[B_x], writes=[Bv, Bsq])
        p.emit("vector", "tensor_scalar_mul", nm, nm, 1.0 / DM,
               reads=[Bn], writes=[Bn], first=False)
        p.emit("vector", "tensor_mul", mr, nm, nm,
               reads=[Bn], writes=[Bmr])
        p.emit("vector", "tensor_scalar", var, var, 1.0 / DM, EPS_LN,
               ALU.mult, ALU.add, reads=[Bv], writes=[Bv], first=False)
        p.emit("vector", "tensor_sub", var, var, mr,
               reads=[Bv, Bmr], writes=[Bv], first=False)
        p.emit("scalar", "activation", rstd, var, AF.Ln,
               reads=[Bv], writes=[Br_])
        p.emit("scalar", "activation", rstd, rstd, AF.Exp, scale=-0.5,
               reads=[Br_], writes=[Br_], first=False)
        p.emit("vector", "tensor_mul", mr, nm, rstd,
               reads=[Bn, Br_], writes=[Bmr])
        p.emit("vector", "tensor_scalar_mul", mr, mr, -1.0,
               reads=[Bmr], writes=[Bmr], first=False)
        p.emit("vector", "tensor_scalar", out_ap, x_ap, rstd, mr,
               ALU.mult, ALU.add,
               reads=[B_x, Br_, Bmr], writes=[B_out])
    tr_slot = [0]

    def transpose_tile(in_tile, out_ap, B_in, B_out):
        for dc in range(8):
            p.emit("tensor", "transpose", P3[:, dc * 128:(dc + 1) * 128],
                   in_tile[:, dc * 128:(dc + 1) * 128], ident[:],
                   reads=[B_in, B_consts], writes=[B_P3],
                   first=(dc == 0), declare_only=(dc < 7))
        p.emit("vector", "tensor_copy", out_ap, P3[:],
               reads=[B_P3], writes=[B_out])

    with ExitStack() as actx:
        asb = lambda *a: actx.enter_context(nc.sbuf_tensor(*a))
        wq_bf = asb("wq_bf", [128, 8, DM], BF16)
        wkv_bf = asb("wkv_bf", [128, 8, 2 * DM], BF16)
        x_g = asb("x_g", [128, 2, 4, DM], F32)
        xn_tm = asb("xn_tm", [128, 4, DM], BF16)
        xn_fm = asb("xn_fm", [128, 2, 4, 8, 128], BF16)
        q_stage = asb("q_stage", [128, 2, 8, 512], BF16)
        k_stage = asb("k_stage", [128, 2, 8, 512], BF16)
        v_stage = asb("v_stage", [128, 2, 4, 16 * 65], BF16)
        vtmp = asb("vtmp", [128, 1024], BF16)

        B_wq, B_wkv = [Buf() for _ in range(8)], [Buf() for _ in range(8)]
        B_xg = [[Buf() for _ in range(4)] for _ in range(2)]
        B_xntm = [Buf() for _ in range(4)]
        B_xnfm = [[Buf() for _ in range(4)] for _ in range(2)]
        B_qs = [[Buf() for _ in range(8)] for _ in range(2)]
        B_ks = [[Buf() for _ in range(8)] for _ in range(2)]
        B_vs = [[Buf() for _ in range(4)] for _ in range(2)]
        B_vtmp = Buf()

        for dc in range(8):
            p.emit("gpsimd", "dma_start", wq_bf[:, dc, :],
                   wq[dc * 128:(dc + 1) * 128, :], writes=[B_wq[dc]],
                   dma="wqkv", declare_only=True)
        for dc in range(8):
            p.emit("gpsimd", "dma_start", wkv_bf[:, dc, :],
                   wkv[dc * 128:(dc + 1) * 128, :], writes=[B_wkv[dc]],
                   dma="wqkv", declare_only=True)
        for dc in range(8):
            B_wq[dc].ready = [("D_wqkv", p.cnt["D_wqkv"])]
            B_wkv[dc].ready = [("D_wqkv", p.cnt["D_wqkv"])]

        B_vones = Buf()
        for b in range(2):
            for h in range(16):
                p.emit("vector", "memset",
                       v_stage[:, b, :, h * 65 + 64:h * 65 + 65], 1.0,
                       writes=[B_vones], inc=(b == 1 and h == 15), first=False)

        def emit_lntr(tg):
            bu = tg % 2
            p.emit("sync", "dma_start", x_g[:, bu, :, :],
                   x_in[tg * 512:(tg + 1) * 512, :].rearrange("(t p) d -> p t d", p=128),
                   writes=B_xg[bu], dma="xg%d" % bu, declare_only=True)
            for tt in range(4):
                B_xg[bu][tt].ready = [("D_xg%d" % bu, p.cnt["D_xg%d" % bu])]
                B_xg[bu][tt].free = []
            for tt in range(4):
                layer_norm(x_g[:, bu, tt, :], xn_tm[:, tt, :], B_xg[bu][tt],
                           B_xntm[tt], tt)
                transpose_tile(xn_tm[:, tt, :], xn_fm[:, bu, tt, :, :],
                               B_xntm[tt], B_xnfm[bu][tt])

        def emit_proj(tg):
            bu = tg % 2
            for kind in range(2):
                wbf, wB = (wq_bf, B_wq) if kind == 0 else (wkv_bf, B_wkv)
                stage, sB = (q_stage, B_qs) if kind == 0 else (k_stage, B_ks)
                for fc in range(8):
                    ps = P0 if fc % 2 == 0 else P1
                    pB = B_P[id(ps)]
                    rd_all = list(wB) + [B_xnfm[bu][t] for t in range(4)]
                    for dc in range(8):
                        first, last = dc == 0, dc == 7
                        rd = rd_all if last else [wB[dc]] + [B_xnfm[bu][t] for t in range(4)]
                        p.emit("tensor", "matmul", ps[:, 0:512],
                               wbf[:, dc, fc * 128:(fc + 1) * 128],
                               xn_fm[:, bu, :, dc, :],
                               start=first, stop=last,
                               reads=rd, writes=[pB],
                               inc=last, declare_only=first and not last, first=first)
                    if kind == 0:
                        p.emit("scalar", "mul", stage[:, bu, fc, :], ps[:, 0:512],
                               DH ** -0.5, reads=[pB], writes=[sB[bu][fc]])
                    else:
                        p.emit("scalar", "copy", stage[:, bu, fc, :], ps[:, 0:512],
                               reads=[pB], writes=[sB[bu][fc]])
            for tt in range(4):
                for half in range(2):
                    hB = B_P4h[half]
                    rd_all = list(B_wkv) + [B_xnfm[bu][tt]]
                    for dc in range(8):
                        first, last = dc == 0, dc == 7
                        p.emit("tensor", "matmul", P4[:, half * 512:(half + 1) * 512],
                               xn_fm[:, bu, tt, dc, :],
                               wkv_bf[:, dc, DM + half * 512:DM + (half + 1) * 512],
                               start=first, stop=last,
                               reads=rd_all if last else [B_wkv[dc], B_xnfm[bu][tt]],
                               writes=[hB],
                               inc=last, declare_only=first and not last, first=first)
                    p.emit("scalar", "copy", vtmp[:, half * 512:(half + 1) * 512],
                           P4[:, half * 512:(half + 1) * 512],
                           reads=[hB], writes=[B_vtmp], first=(half == 0),
                           declare_only=(half == 0))
                p.emit("vector", "tensor_copy",
                       v_stage[:, bu, tt, :].rearrange("p (h c) -> p h c", c=65)[:, :, 0:64],
                       vtmp[:].rearrange("p (h c) -> p h c", c=64),
                       reads=[B_vtmp], writes=[B_vs[bu][tt]])
            p.emit("scalar", "dma_start", q_hbm[tg], q_stage[:, bu, :, :],
                   reads=[B_qs[bu][fc] for fc in range(8)], dma="qst%d" % bu)
            p.emit("scalar", "dma_start",
                   k_hbm[:, :, tg * 512:(tg + 1) * 512].rearrange("fc p c -> p fc c"),
                   k_stage[:, bu, :, :],
                   reads=[B_ks[bu][fc] for fc in range(8)], dma="kst%d" % bu)
            p.emit("scalar", "dma_start",
                   v_hbm[tg * 4:(tg + 1) * 4].rearrange("kt p c -> p kt c"),
                   v_stage[:, bu, :, :],
                   reads=B_vs[bu] + [B_vones], dma="vst%d" % bu)

        for tg in range(NG):
            emit_lntr(tg)
            if tg >= 1:
                emit_proj(tg - 1)
        emit_proj(NG - 1)

        p.barrier("A->B")

    with ExitStack() as bctx:
        bsb = lambda *a: bctx.enter_context(nc.sbuf_tensor(*a))
        wo_bf = bsb("wo_bf", [128, 8, DM], BF16)
        v_sb = bsb("v_sb", [128, NT, 16 * 65], BF16)
        k_buf = bsb("k_buf", [128, 2, N], BF16)
        q_buf = bsb("q_buf", [128, 2, 512], BF16)
        pT_sb = bsb("pT_sb", [128, 4096], BF16)
        attn_fm = bsb("attn_fm", [128, 8, 512], BF16)
        attn_f32 = bsb("attn_f32", [128, 512], F32)
        tmp_bf = bsb("tmp_bf", [128, 512], BF16)
        xg2 = bsb("xg2", [128, 4, DM], F32)
        x2_sb = bsb("x2_sb", [128, 4, DM], F32)
        xn2_tm = bsb("xn2_tm", [128, 4, DM], BF16)
        xn2_fm = bsb("xn2_fm", [128, 4, 8, 128], BF16)

        B_wo = [Buf() for _ in range(8)]
        B_vc = [Buf() for _ in range(4)]
        B_kb = [Buf(), Buf()]
        B_qb = [Buf(), Buf()]
        B_pT = [Buf() for _ in range(8)]
        B_afm = [[Buf(), Buf()] for _ in range(8)]
        B_af32 = Buf()
        B_tmpbf = Buf()
        B_xg2 = [Buf() for _ in range(4)]
        B_x2 = [Buf() for _ in range(4)]
        B_xn2tm = [Buf() for _ in range(4)]
        B_xn2fm = [[Buf() for _ in range(4)] for _ in range(8)]

        for dc in range(8):
            p.emit("gpsimd", "dma_start", wo_bf[:, dc, :],
                   wo[dc * 128:(dc + 1) * 128, :], writes=[B_wo[dc]],
                   dma="wo", declare_only=True)
        for dc in range(8):
            B_wo[dc].ready = [("D_wo", p.cnt["D_wo"])]
        def load_kq(qc, fc):
            par = fc % 2
            nkb = 4 * qc + 4
            p.emit("sync", "dma_start", k_buf[:, par, 0:nkb * 128],
                   k_hbm[fc, :, 0:nkb * 128], writes=[B_kb[par]],
                   dma="kq%d" % par, declare_only=True)
            p.emit("sync", "dma_start", q_buf[:, par, :],
                   q_hbm[qc, :, fc, :], writes=[B_qb[par]],
                   dma="kq%d" % par, declare_only=True)
            tot = [("D_kq%d" % par, p.cnt["D_kq%d" % par])]
            for b in (B_kb[par], B_qb[par]):
                b.ready = list(tot)
                b.free = []

        load_kq(0, 0)
        VCH = max(1, NT // 4)
        for vc in range(4):
            lo, hi = vc * VCH, min((vc + 1) * VCH, NT)
            if lo >= hi:
                B_vc[vc].ready = list(B_vc[vc - 1].ready)
                continue
            p.emit("sync", "dma_start", v_sb[:, lo:hi, :],
                   v_hbm[lo:hi].rearrange("kt p c -> p kt c"),
                   writes=[B_vc[vc]], dma="vl%d" % vc)
        zero_sb = bsb("zero_sb", [128, DM], BF16)
        B_zero = Buf()
        B_zx = Buf()
        p.emit("vector", "memset", zero_sb[:], 0.0, writes=[B_zero])
        nzt = NE * CAP // 128 + 1
        for zi in range(nzt):
            p.emit("scalar", "dma_start", xs_hbm[zi * 128:(zi + 1) * 128, :],
                   zero_sb[:], reads=[B_zero], dma="zx", declare_only=True)
        p.emit("scalar", "dma_start", ys_hbm[NE * CAP:NE * CAP + 128, :],
               zero_sb[:], reads=[B_zero], dma="zx", declare_only=True)
        B_zx.ready = [("D_zx", p.cnt["D_zx"])]
        B_zx_holder[0] = B_zx


        pend_epi = [None]
        pend_osw = [None]
        for qc in range(NG):
            nkb = 4 * qc + 4
            p.emit("gpsimd", "dma_start", xg2[:],
                   x_in[qc * 512:(qc + 1) * 512, :].rearrange("(t p) d -> p t d", p=128),
                   writes=B_xg2, dma="xb0", declare_only=True)
            for tt in range(4):
                B_xg2[tt].ready = [("D_xb0", p.cnt["D_xb0"])]
                B_xg2[tt].free = []
            for fc in range(8):
                par = fc % 2
                nfc, nqc = fc + 1, qc
                if nfc == 8:
                    nfc, nqc = 0, qc + 1
                if nqc < NG:
                    load_kq(nqc, nfc)
                SPS = (P0, P1)
                nkp = nkb // 2

                def emit_s(h2, kb, j):
                    ps = SPS[h2]
                    p.emit("tensor", "matmul", ps[:, j * 512:(j + 1) * 512],
                           k_buf[h2 * 64:h2 * 64 + 64, par, kb * 128:(kb + 1) * 128],
                           q_buf[h2 * 64:h2 * 64 + 64, par, :],
                           start=True, stop=True,
                           reads=[B_kb[par], B_qb[par]], writes=[B_P[id(ps)]],
                           inc=(j == 1), declare_only=(j == 0), first=(j == 0))

                def emit_exp(h2, pp):
                    ps = SPS[h2]
                    s0 = 4 * h2 + (2 * pp) % 4
                    p.emit("scalar", "activation",
                           pT_sb[:, s0 * 512:(s0 + 2) * 512], ps[:], AF.Exp,
                           reads=[B_P[id(ps)]], writes=[B_pT[s0], B_pT[s0 + 1]])
                    for j in range(2):
                        kb = 2 * pp + j
                        d = kb - 4 * qc
                        if d >= 0:
                            sl = 4 * h2 + kb % 4
                            p.emit("vector", "tensor_mul",
                                   pT_sb[:, sl * 512:(sl + 1) * 512],
                                   pT_sb[:, sl * 512:(sl + 1) * 512],
                                   maskd[:, d, :],
                                   reads=[B_pT[sl], B_consts],
                                   writes=[B_pT[sl]], first=False)

                pv_ps_l = (P2[0:65, :], P4[0:65, 0:512])
                pv_B_l = (B_P[id(P2)], B_P4h[0])

                def emit_pv(h2, kb):
                    sl = 4 * h2 + kb % 4
                    first, last = kb == 0, kb == nkb - 1
                    odd = kb % 2 == 1
                    rd = [B_vc[min(3, kb // VCH)], B_pT[sl]]
                    if odd:
                        rd.append(B_pT[4 * h2 + (kb - 1) % 4])
                    p.emit("tensor", "matmul", pv_ps_l[h2],
                           v_sb[:, kb, (2 * fc + h2) * 65:(2 * fc + h2) * 65 + 65],
                           pT_sb[:, sl * 512:(sl + 1) * 512],
                           start=first, stop=last,
                           reads=rd, writes=[pv_B_l[h2]],
                           inc=odd, declare_only=not last, first=first)

                for pp in range(nkp):
                    for j in range(2):
                        for h2 in range(2):
                            emit_s(h2, 2 * pp + j, j)
                    for h2 in range(2):
                        emit_exp(h2, pp)
                    if pp == 0 and fc == 2 and pend_osw[0] is not None:
                        pend_osw[0]()
                        pend_osw[0] = None
                    if pp == 1 and pend_epi[0] is not None:
                        pend_epi[0]()
                        pend_epi[0] = None
                    if pp >= 1:
                        for h2 in range(2):
                            emit_pv(h2, 2 * (pp - 1))
                            emit_pv(h2, 2 * (pp - 1) + 1)
                for h2 in range(2):
                    emit_pv(h2, nkb - 2)
                    emit_pv(h2, nkb - 1)

                for h2 in range(2):
                    lc = slice(h2 * 512, h2 * 512 + 512)
                    p.emit("vector", "tensor_scalar_add", lr_sb[64:65, lc],
                           pv_ps_l[h2][64:65, :], EPS_ATT,
                           reads=[pv_B_l[h2]], writes=[B_l[h2]])
                    p.emit("scalar", "activation", lr_sb[64:65, lc],
                           lr_sb[64:65, lc], AF.Ln,
                           reads=[B_l[h2]], writes=[B_l[h2]], first=False)
                    p.emit("scalar", "activation", r_bf[64:65, lc],
                           lr_sb[64:65, lc], AF.Exp, scale=-1.0,
                           reads=[B_l[h2]], writes=[B_r[h2]])

                def make_epilogue(fc=fc):
                    def epi():
                        for h2 in range(2):
                            lc = slice(h2 * 512, h2 * 512 + 512)
                            p.emit("tensor", "matmul", P4[0:64, 512:1024],
                                   ones_bf[64:65, :], r_bf[64:65, lc],
                                   start=True, stop=True,
                                   reads=[B_r[h2], B_consts], writes=[B_P4h[1]])
                            p.emit("vector", "tensor_copy", attn_f32[0:64, :],
                                   pv_ps_l[h2][0:64, :],
                                   reads=[pv_B_l[h2]], writes=[B_af32])
                            if h2 == 0:
                                p.emit("vector", "tensor_mul",
                                       attn_fm[0:64, fc, :],
                                       attn_f32[0:64, :], P4[0:64, 512:1024],
                                       reads=[B_af32, B_P4h[1]],
                                       writes=[B_afm[fc][0]])
                            else:
                                p.emit("vector", "tensor_mul", tmp_bf[0:64, :],
                                       attn_f32[0:64, :], P4[0:64, 512:1024],
                                       reads=[B_af32, B_P4h[1]], writes=[B_tmpbf])
                                p.emit("tensor", "matmul", P4[64:128, 0:512],
                                       ident[0:64, 0:64], tmp_bf[0:64, :],
                                       start=True, stop=True, tile_position=(0, 64),
                                       reads=[B_tmpbf, B_consts], writes=[B_P4h[0]])
                                p.emit("vector", "tensor_copy",
                                       attn_fm[64:128, fc, :], P4[64:128, 0:512],
                                       reads=[B_P4h[0]], writes=[B_afm[fc][1]])
                    return epi
                pend_epi[0] = make_epilogue()
            if pend_epi[0] is not None:
                pend_epi[0]()
                pend_epi[0] = None
            for qt in range(4):
                for half in range(2):
                    hB = B_P4h[half]
                    rd_all = list(B_wo) + [B_afm[f][j] for f in range(8) for j in range(2)]
                    for fc in range(8):
                        first, last = fc == 0, fc == 7
                        p.emit("tensor", "matmul", P4[:, half * 512:(half + 1) * 512],
                               attn_fm[:, fc, qt * 128:(qt + 1) * 128],
                               wo_bf[:, fc, half * 512:(half + 1) * 512],
                               start=first, stop=last,
                               reads=rd_all if last else
                               [B_wo[fc], B_afm[fc][0], B_afm[fc][1]],
                               writes=[hB],
                               inc=last, declare_only=first and not last, first=first)
                    sl = slice(half * 512, (half + 1) * 512)
                    p.emit("vector", "tensor_add", x2_sb[:, qt, sl],
                           P4[:, sl], bo_sb[:, sl],
                           reads=[hB, B_consts], writes=[B_x2[qt]])
                    p.emit("vector", "tensor_add", x2_sb[:, qt, sl],
                           x2_sb[:, qt, sl], xg2[:, qt, sl],
                           reads=[B_xg2[qt], B_x2[qt]],
                           writes=[B_x2[qt]], first=False)
                layer_norm(x2_sb[:, qt, :], xn2_tm[:, qt, :], B_x2[qt],
                           B_xn2tm[qt], qt)
            def make_osweep3(qc=qc):
                def osw3():
                    for qt in range(4):
                        transpose_tile(xn2_tm[:, qt, :], xn2_fm[:, qt, :, :],
                                       B_xn2tm[qt], B_xn2fm[0][qt])
                        rt_B = B_P4h[1]
                        rt_rd = [B_xn2fm[0][qt], B_wr]
                        for dc in range(8):
                            first, last = dc == 0, dc == 7
                            p.emit("tensor", "matmul", P4[:, 512:520],
                                   xn2_fm[:, qt, dc, :],
                                   wr_bf[:, dc, :],
                                   start=first, stop=last,
                                   reads=rt_rd if last else [B_xn2fm[0][qt], B_wr],
                                   writes=[rt_B],
                                   inc=last, declare_only=first and not last,
                                   first=first)
                        p.emit("vector", "tensor_add", lg_sb[:, qt, :],
                               P4[:, 512:520], br_sb[:],
                               reads=[rt_B, B_consts], writes=[B_rt["lg"][qt]])
                        p.emit("vector", "reduce_max", lmax_sb[:, qt:qt + 1],
                               lg_sb[:, qt, :], AX.X,
                               reads=[B_rt["lg"][qt]], writes=[B_rt["lmax"][qt]])
                        p.emit("vector", "tensor_scalar", sub_sb[:, qt, :],
                               lg_sb[:, qt, :], lmax_sb[:, qt:qt + 1], None,
                               ALU.subtract,
                               reads=[B_rt["lg"][qt], B_rt["lmax"][qt]],
                               writes=[B_rt["sub"][qt]])
                        p.emit("scalar", "sign", sgn_sb[:, qt, :], sub_sb[:, qt, :],
                               reads=[B_rt["sub"][qt]], writes=[B_rt["sgn"][qt]])
                        p.emit("scalar", "activation", pe_sb[:, qt, :],
                               sub_sb[:, qt, :], AF.Exp,
                               reads=[B_rt["sub"][qt]], writes=[B_rt["pe"][qt]])
                        p.emit("vector", "reduce_sum", den_sb[:, qt:qt + 1],
                               pe_sb[:, qt, :], AX.X,
                               reads=[B_rt["pe"][qt]], writes=[B_rt["den"][qt]])
                        ti = qc * 4 + qt
                        p.emit("vector", "reciprocal", gates_sb[:, ti:ti + 1],
                               den_sb[:, qt:qt + 1],
                               reads=[B_rt["den"][qt]], writes=[B_gate[ti]])
                        p.emit("vector", "tensor_scalar_add", mask_bf[:, qt, :],
                               sgn_sb[:, qt, :], 1.0,
                               reads=[B_rt["sgn"][qt]], writes=[B_maskbf[qt]])
                        s = tr_slot[0] % 8
                        tr_slot[0] += 1
                        p.emit("tensor", "transpose", P3[0:8, s * 128:(s + 1) * 128],
                               mask_bf[:, qt, :], ident[:],
                               reads=[B_maskbf[qt], B_consts], writes=[B_P3])
                        p.emit("vector", "tensor_copy",
                               maskT_sb[0:8, ti * 128:(ti + 1) * 128],
                               P3[0:8, s * 128:s * 128 + 128],
                               reads=[B_P3], writes=[B_maskT[ti]])
                    p.emit("scalar", "dma_start",
                           xn2tm_hbm[qc * 512:(qc + 1) * 512, :].rearrange(
                               "(t p) d -> p t d", p=128),
                           xn2_tm[:],
                           reads=[B_xn2tm[qt] for qt in range(4)], dma="xn2st")
                return osw3
            pend_osw[0] = make_osweep3()
            p.emit("scalar", "dma_start",
                   x2_hbm[qc * 512:(qc + 1) * 512, :].rearrange("(t p) d -> p t d", p=128),
                   x2_sb[:], reads=B_x2, dma="x2st")

        if pend_osw[0] is not None:
            pend_osw[0]()
            pend_osw[0] = None
        p.barrier("B->R")

    with ExitStack() as rctx:
        rsb = lambda *a, **k: rctx.enter_context(nc.sbuf_tensor(*a, **k))
        cums = rsb("cums", [8, N], F32)
        ones8_sb = rsb("ones8_sb", [8, 1], F32)
        basem1_sb = rsb("basem1_sb", [8, 1], F32)
        slots_f = rsb("slots_f", [128, NT], F32)
        xtile = rsb("xtile", [128, 2, DM], BF16)
        B_cums = Buf()
        B_slotf = Buf()
        B_small = Buf()
        B_xt = [Buf(), Buf()]
        B_zx = B_zx_holder[0]

        p.emit("sync", "dma_start", ones8_sb[:], ones8_in, writes=[B_small],
               dma="r8", declare_only=True)
        p.emit("sync", "dma_start", basem1_sb[:], basem1_in, writes=[B_small],
               dma="r8", declare_only=True)
        B_small.ready = [("D_r8", p.cnt["D_r8"])]

        p.emit("vector", "tensor_tensor_scan", cums[:], maskT_sb[:], maskT_sb[:],
               0.0, ALU.add, ALU.bypass,
               reads=B_maskT, writes=[B_cums])
        p.emit("vector", "tensor_scalar_add", cums[:], cums[:],
               basem1_sb[:, 0:1], reads=[B_cums, B_small], writes=[B_cums],
               first=False)
        p.emit("vector", "tensor_mul", cums[:], cums[:], maskT_sb[:],
               reads=[B_cums] + B_maskT, writes=[B_cums], first=False)
        for ti in range(NT):
            p.emit("tensor", "matmul", P2[:, ti:ti + 1],
                   cums[:, ti * 128:(ti + 1) * 128], ones8_sb[:],
                   start=True, stop=True,
                   reads=[B_cums, B_small], writes=[B_P[id(P2)]],
                   inc=(ti == NT - 1), declare_only=(ti < NT - 1),
                   first=(ti == 0))
        p.emit("vector", "tensor_scalar_min", slots_f[:], P2[:, 0:NT],
               float(NE * CAP), reads=[B_P[id(P2)]], writes=[B_slotf])
        p.emit("vector", "tensor_copy", slot_sb[:], slots_f[:],
               reads=[B_slotf], writes=[B_slot])
        for ti in range(NT):
            bu = ti % 2
            p.emit("sync", "dma_start", xtile[:, bu, :],
                   xn2tm_hbm[ti * 128:(ti + 1) * 128, :],
                   writes=[B_xt[bu]], dma="xt%d" % bu)
            p.emit("gpsimd", "indirect_dma_start", xs_hbm,
                   IndirectOffsetOnAxis(slot_sb[:, ti:ti + 1], 0),
                   xtile[:, bu, :], None,
                   reads=[B_xt[bu], B_slot, B_zx], dma="sc%d" % bu)

        p.barrier("R->C")

    ab_ctx.close()

    CGRP = [(o, min(512, CAP - o)) for o in range(0, CAP, 512)]
    NTE = CAP // 128
    with ExitStack() as cctx:
        csb = lambda *a: cctx.enter_context(nc.sbuf_tensor(*a))
        w1_bf = csb("w1_bf", [128, 2, 8, HFF], BF16)
        w2_bf = csb("w2_bf", [128, 16, DM], BF16)
        xsf_sb = csb("xsf_sb", [128, 3, 8, 512], BF16)
        h_buf = csb("h_buf", [128, 16, 512], BF16)
        y_tmp = csb("y_tmp", [128, DM], F32)
        ys_bf = csb("ys_bf", [128, 2, DM], BF16)
        b1_sb = csb("b1_sb", [128, 16], F32)
        b2_sb = csb("b2_sb", [128, DM], F32)

        B_w1 = [[Buf() for _ in range(8)] for _ in range(2)]
        B_w2 = [Buf() for _ in range(16)]
        B_xsf = [Buf(), Buf(), Buf()]
        B_h = [Buf() for _ in range(16)]
        B_yt = Buf()
        B_ysb = [Buf(), Buf()]
        B_b1 = Buf()
        B_b2 = Buf()

        def load_expert(e):
            wp = e % 2
            k1, k2 = "w1e%d" % wp, "w2e%d" % wp
            for dc in range(8):
                p.emit("gpsimd", "dma_start", w1_bf[:, wp, dc, :],
                       w1[e, dc * 128:(dc + 1) * 128, :], writes=[B_w1[wp][dc]],
                       dma=k1, declare_only=True)
            p.emit("gpsimd", "dma_start", b1_sb[:], b1_l[e], writes=[B_b1],
                   dma=k1, declare_only=True)
            tot1 = [("D_" + k1, p.cnt["D_" + k1])]
            for dc in range(8):
                B_w1[wp][dc].ready = list(tot1)
            B_b1.ready = list(tot1)
            for hc in range(16):
                p.emit("gpsimd", "dma_start", w2_bf[:, hc, :],
                       w2[e, hc * 128:(hc + 1) * 128, :], writes=[B_w2[hc]],
                       dma=k2, declare_only=True)
            p.emit("gpsimd", "dma_start", b2_sb[:], b2_b[e], writes=[B_b2],
                   dma=k2, declare_only=True)
            tot2 = [("D_" + k2, p.cnt["D_" + k2])]
            for hc in range(16):
                B_w2[hc].ready = list(tot2)
            B_b2.ready = list(tot2)

        xs_count = [0]
        ys_count = [0]

        def load_xs(e, gi):
            off, ln = CGRP[gi]
            bu = xs_count[0] % 3
            xs_count[0] += 1
            for dc in range(8):
                p.emit("sync", "dma_start_transpose",
                       xsf_sb[:, bu, dc, 0:ln],
                       xs_hbm[e * CAP + off:e * CAP + off + ln,
                              dc * 128:(dc + 1) * 128],
                       writes=[B_xsf[bu]], dma="xf%d" % bu, declare_only=True)
            B_xsf[bu].ready = [("D_xf%d" % bu, p.cnt["D_xf%d" % bu])]
            B_xsf[bu].free = []

        NGRP = len(CGRP)
        load_expert(0)
        load_xs(0, 0)
        if NGRP > 1:
            load_xs(0, 1)
        else:
            load_xs(1, 0)
        xs_use = [0]
        for e in range(NE):
            for gi, (off, ln) in enumerate(CGRP):
                bu = xs_use[0] % 3
                xs_use[0] += 1
                g2 = e * NGRP + gi + 2
                if g2 < NE * NGRP:
                    load_xs(g2 // NGRP, g2 % NGRP)
                wp = e % 2
                for hc in range(16):
                    ps = P0 if hc % 2 == 0 else P1
                    pB = B_P[id(ps)]
                    h_rd = list(B_w1[wp]) + [B_xsf[bu]]
                    for dc in range(8):
                        first, last = dc == 0, dc == 7
                        p.emit("tensor", "matmul", ps[:, 0:ln],
                               w1_bf[:, wp, dc, hc * 128:(hc + 1) * 128],
                               xsf_sb[:, bu, dc, 0:ln],
                               start=first, stop=last,
                               reads=h_rd if last else [B_w1[wp][dc], B_xsf[bu]],
                               writes=[pB],
                               inc=last, declare_only=first and not last,
                               first=first)
                    p.emit("scalar", "activation", h_buf[:, hc, 0:ln],
                           ps[:, 0:ln], AF.Relu, bias=b1_sb[:, hc:hc + 1],
                           reads=[pB, B_b1], writes=[B_h[hc]])
                for qt in range(ln // 128):
                    bu2 = ys_count[0] % 2
                    ys_count[0] += 1
                    for half in range(2):
                        hB = B_P4h[half]
                        y_rd = list(B_h) + list(B_w2)
                        for hc in range(16):
                            first, last = hc == 0, hc == 15
                            p.emit("tensor", "matmul",
                                   P4[:, half * 512:(half + 1) * 512],
                                   h_buf[:, hc, qt * 128:(qt + 1) * 128],
                                   w2_bf[:, hc, half * 512:(half + 1) * 512],
                                   start=first, stop=last,
                                   reads=y_rd if last else [B_h[hc], B_w2[hc]],
                                   writes=[hB],
                                   inc=last, declare_only=first and not last,
                                   first=first)
                        sl = slice(half * 512, (half + 1) * 512)
                        p.emit("vector", "tensor_add", y_tmp[:, sl],
                               P4[:, sl], b2_sb[:, sl],
                               reads=[hB, B_b2], writes=[B_yt],
                               first=(half == 0))
                    p.emit("scalar", "copy", ys_bf[:, bu2, :], y_tmp[:],
                           reads=[B_yt], writes=[B_ysb[bu2]])
                    row = e * CAP + off + qt * 128
                    p.emit("scalar", "dma_start",
                           ys_hbm[row:row + 128, :],
                           ys_bf[:, bu2, :], reads=[B_ysb[bu2]], dma="yso%d" % bu2)
            if e + 1 < NE:
                load_expert(e + 1)

        ys_done = [(s, v) for s, v in p.cnt.items() if s.startswith("D_yso")]

        with ExitStack() as fctx:
            fsb = lambda *a: fctx.enter_context(nc.sbuf_tensor(*a))
            x2g = fsb("x2g", [128, 4, DM], F32)
            y_stage = fsb("y_stage", [128, 4, DM], F32)
            yg_sb = fsb("yg_sb", [128, 2, DM], BF16)
            ygs_sb = fsb("ygs_sb", [128, 2, DM], F32)
            B_x2g = Buf()
            B_ys = Buf()
            B_yg = [Buf(), Buf()]
            B_ygs = [Buf(), Buf()]
            for qc in range(NG):
                p.emit("sync", "dma_start", x2g[:],
                       x2_hbm[qc * 512:(qc + 1) * 512, :].rearrange(
                           "(t p) d -> p t d", p=128),
                       writes=[B_x2g], dma="x2g0")
                for qt in range(4):
                    ti = qc * 4 + qt
                    tb = ti % 2
                    p.emit("gpsimd", "indirect_dma_start", yg_sb[:, tb, :], None,
                           ys_hbm, IndirectOffsetOnAxis(slot_sb[:, ti:ti + 1], 0),
                           reads=[B_slot], writes=[B_yg[tb]], dma="gb%d" % tb,
                           extra=ys_done)
                    p.emit("scalar", "activation", ygs_sb[:, tb, :],
                           yg_sb[:, tb, :], AF.Copy,
                           scale=gates_sb[:, ti:ti + 1],
                           reads=[B_yg[tb], B_gate[ti]], writes=[B_ygs[tb]])
                    p.emit("vector", "tensor_add", y_stage[:, qt, :],
                           x2g[:, qt, :], ygs_sb[:, tb, :],
                           reads=[B_x2g, B_ygs[tb]], writes=[B_ys],
                           first=(qt == 0), declare_only=(qt < 3))
                p.emit("scalar", "dma_start",
                       y_out[qc * 512:(qc + 1) * 512, :].rearrange(
                           "(t p) d -> p t d", p=128),
                       y_stage[:], reads=[B_ys], dma="yst0")

    final_waits = [(s, v) for s, v in p.cnt.items() if s.startswith("D_")]
    p.wait_only("sync", final_waits)
    p.wait_only("scalar", final_waits)

    with ctx:
        sems = {}
        with ExitStack() as sctx:
            for name in p.cnt:
                sems[name] = sctx.enter_context(nc.semaphore(name))
            with nc.Block() as block:
                p.realize(block, sems)
    return nc


_NC_CACHE = {}


def _get_nc(debug=False):
    key = bool(debug)
    if key not in _NC_CACHE:
        _NC_CACHE[key] = build_program(debug=debug)
    return _NC_CACHE[key]


def _prep_inputs(inputs):
    g = {k: np.asarray(v) for k, v in inputs.items()}
    x = np.ascontiguousarray(g["L0_attn_x"], dtype=np.float32)
    wq = np.ascontiguousarray(g["L0_attn_Wq"], dtype=np.float32)
    wkv = np.ascontiguousarray(g["L0_attn_Wkv"], dtype=np.float32)
    wo = np.ascontiguousarray(g["L0_attn_Wo"], dtype=np.float32)
    bo = np.ascontiguousarray(g["L0_attn_bo"], dtype=np.float32)
    wr = np.ascontiguousarray(g["L0_ffn_Wr"], dtype=np.float32)
    br = np.ascontiguousarray(g["L0_ffn_br"], dtype=np.float32)
    w1 = np.ascontiguousarray(g["L0_ffn_W1"], dtype=np.float32)
    b1 = np.ascontiguousarray(g["L0_ffn_b1"], dtype=np.float32)
    w2 = np.ascontiguousarray(g["L0_ffn_W2"], dtype=np.float32)
    b2 = np.ascontiguousarray(g["L0_ffn_b2"], dtype=np.float32)

    bo_b = np.ascontiguousarray(np.broadcast_to(bo[None, :], (128, DM)))
    br_b = np.ascontiguousarray(np.broadcast_to(br[None, :], (128, NE)))
    b1_l = np.ascontiguousarray(b1.reshape(NE, 16, 128).transpose(0, 2, 1))
    b2_b = np.ascontiguousarray(np.broadcast_to(b2[:, None, :], (NE, 128, DM)))

    kk = np.arange(128)[:, None]
    qq = np.arange(512)[None, :]
    maskd = np.stack([(qq - 128 * d >= kk) for d in range(4)], axis=1)
    maskd = maskd.astype(ml_dtypes.bfloat16)
    ident = np.eye(128, dtype=ml_dtypes.bfloat16)
    ones_b = np.ones((128, 64), dtype=np.float32)
    ones8 = np.ones((8, 1), dtype=np.float32)
    cap = max(256, ((N * 5 // 16) + 127) // 128 * 128)
    basem1 = (np.arange(NE, dtype=np.float32)[:, None] * cap) - 1.0

    shared = dict(wq=wq, wkv=wkv, wo=wo, bo_b=bo_b, wr=wr, br_b=br_b,
                  w1=w1, b1_l=b1_l, w2=w2, b2_b=b2_b,
                  maskd=np.ascontiguousarray(maskd), ident=ident, ones_b=ones_b,
                  ones8=ones8, basem1=basem1)
    in_maps = []
    for b in range(8):
        m = dict(shared)
        m["x_in"] = np.ascontiguousarray(x[b])
        in_maps.append(m)
    return in_maps


def kernel(**inputs):
    in_maps = _prep_inputs(inputs)
    nc = _get_nc(debug=False)
    res = run_bass_kernel_spmd(nc, in_maps, core_ids=list(range(8)))
    out = np.stack([np.asarray(r["y_out"], dtype=np.float32) for r in res.results])
    return out


# version 28
# speedup vs baseline: 1.0542x; 1.0310x over previous
import numpy as np
import ml_dtypes
import concourse.bass as bass
import concourse.mybir as mybir
from concourse.bass_utils import run_bass_kernel_spmd
from concourse.bass_core_rust import IndirectOffsetOnAxis
from contextlib import ExitStack

F32 = mybir.dt.float32
BF16 = mybir.dt.bfloat16
AF = mybir.ActivationFunctionType
ALU = mybir.AluOpType
AX = mybir.AxisListType

N, DM, NHEAD, DH, NE, HFF = 4096, 1024, 16, 64, 8, 2048
NT, NG = 32, 8
EPS_LN = 1e-5
EPS_ATT = 1e-8

ENGS = ("sync", "scalar", "vector", "tensor", "gpsimd")


class Buf:
    __slots__ = ("ready", "free")

    def __init__(self):
        self.ready = []
        self.free = []


class Prog:

    def __init__(self):
        self.recs = {e: [] for e in ENGS}
        self.cnt = {}
        for e in ENGS:
            self.cnt["e_" + e] = 0
        self.hwm = {e: {} for e in ENGS}

    def _coalesce(self, eng, waits):
        best = {}
        for s, v in waits:
            if v > best.get(s, 0):
                best[s] = v
        out = []
        for s, v in best.items():
            assert v <= self.cnt[s], f"wait on future value of {s}: {v} > {self.cnt[s]}"
            if self.hwm[eng].get(s, 0) < v:
                self.hwm[eng][s] = v
                out.append((s, v))
        return out

    def emit(self, eng, method, *args, reads=(), writes=(), extra=(), dma=None,
             inc=True, declare_only=False, first=True, **kwargs):
        waits = list(extra)
        for b in reads:
            waits += b.ready
        if first:
            for b in writes:
                waits += b.free + b.ready
        waits = self._coalesce(eng, waits)
        if dma is not None:
            sem = "D_" + dma
            if sem not in self.cnt:
                self.cnt[sem] = 0
        else:
            sem = "e_" + eng
        if inc:
            self.cnt[sem] += 16 if dma else 1
            val = self.cnt[sem]
        self.recs[eng].append((waits, method, args, kwargs,
                               sem if inc else None, 16 if dma else 1))
        if inc:
            if not declare_only:
                for b in writes:
                    b.ready = [(sem, val)]
                    b.free = []
            for b in reads:
                b.free.append((sem, val))
        return None

    def wait_only(self, eng, extra):
        waits = self._coalesce(eng, list(extra))
        if waits:
            self.recs[eng].append((waits, None, (), {}, None, 0))

    def barrier(self, tag=""):
        targets = []
        for e in ENGS:
            self.emit(e, "nop", extra=[("e_" + e, self.cnt["e_" + e])])
            targets.append(("e_" + e, self.cnt["e_" + e]))
        for s, v in self.cnt.items():
            if s.startswith("D_"):
                targets.append((s, v))
        for e in ENGS:
            self.wait_only(e, targets)

    def realize(self, block, sems):
        for eng in ENGS:
            recs = self.recs[eng]
            if not recs:
                continue

            def body(E, recs=recs):
                for waits, method, args, kwargs, sem, n in recs:
                    for s, v in waits:
                        E.wait_ge(sems[s], v)
                    if method is None:
                        continue
                    inst = getattr(E, method)(*args, **kwargs)
                    if sem is not None:
                        inst.then_inc(sems[sem], n)
            getattr(block, eng)(body)


def build_program(debug=False):
    nc = bass.Bass("TRN2", target_bir_lowering=False)
    dram = {}

    def din(name, shape, dtype=F32):
        dram[name] = nc.dram_tensor(name, shape, dtype, kind="ExternalInput").ap()
        return dram[name]

    def dscr(name, shape, dtype):
        kind = "ExternalOutput" if debug else "Internal"
        dram[name] = nc.dram_tensor(name, shape, dtype, kind=kind).ap()
        return dram[name]

    x_in = din("x_in", [N, DM])
    wq = din("wq", [DM, DM])
    wkv = din("wkv", [DM, 2 * DM])
    wo = din("wo", [DM, DM])
    bo_b = din("bo_b", [128, DM])
    wr = din("wr", [DM, NE])
    br_b = din("br_b", [128, NE])
    w1 = din("w1", [NE, DM, HFF])
    b1_l = din("b1_l", [NE, 128, 16])
    w2 = din("w2", [NE, HFF, DM])
    b2_b = din("b2_b", [NE, 128, DM])
    maskd_in = din("maskd", [128, 4, 512], BF16)
    ident_in = din("ident", [128, 128], BF16)
    ones_in = din("ones_b", [128, 64])
    ones8_in = din("ones8", [8, 1])
    basem1_in = din("basem1", [8, 1])

    y_out = nc.dram_tensor("y_out", [N, DM], F32, kind="ExternalOutput").ap()

    q_hbm = dscr("q_hbm", [NG, 128, 8, 512], BF16)
    k_hbm = dscr("k_hbm", [8, 128, N], BF16)
    v_hbm = dscr("v_hbm", [NT, 128, 16 * 65], BF16)
    x2_hbm = dscr("x2_hbm", [N, DM], F32)
    xn2tm_hbm = dscr("xn2tm_hbm", [N, DM], BF16)
    CAP = max(256, ((N * 5 // 16) + 127) // 128 * 128)
    xs_hbm = dscr("xs_hbm", [NE * CAP + 128, DM], BF16)
    ys_hbm = dscr("ys_hbm", [NE * CAP + 128, DM], BF16)

    p = Prog()
    ctx = ExitStack()
    B_zx_holder = [None]

    sb = lambda *a, **k: ctx.enter_context(nc.sbuf_tensor(*a, **k))
    ident = sb("ident_s", [128, 128], BF16)
    maskd = sb("maskd_s", [128, 4, 512], BF16)
    ones_sb = sb("ones_sb", [128, 64], F32)
    ones_bf = sb("ones_bf", [128, 64], BF16)
    r_bf = sb("r_bf", [128, 1024], BF16)
    bo_sb = sb("bo_sb", [128, DM], F32)
    br_sb = sb("br_sb", [128, NE], F32)
    wr_bf = sb("wr_bf", [128, 8, NE], BF16)
    gates_sb = sb("gates_sb", [128, NT], F32)
    slot_sb = sb("slot_sb", [128, NT], mybir.dt.int32)
    lg_sb = sb("lg_sb", [128, 4, NE], F32)
    lmax_sb = sb("lmax_sb", [128, 4], F32)
    sub_sb = sb("sub_sb", [128, 4, NE], F32)
    sgn_sb = sb("sgn_sb", [128, 4, NE], F32)
    pe_sb = sb("pe_sb", [128, 4, NE], F32)
    den_sb = sb("den_sb", [128, 4], F32)
    gate_sb = sb("gate_sb", [128, 4], F32)

    ab_ctx = ExitStack()
    absb = lambda *a, **k: ab_ctx.enter_context(nc.sbuf_tensor(*a, **k))
    nm_sb = absb("nm_sb", [128, 4], F32)
    var_sb = absb("var_sb", [128, 4], F32)
    rstd_sb = absb("rstd_sb", [128, 4], F32)
    mr_sb = absb("mr_sb", [128, 4], F32)
    sq_sb = absb("sq_sb", [128, 2, DM], F32)
    lr_sb = absb("lr_sb", [128, 1024], F32)
    maskT_sb = absb("maskT_sb", [8, N], BF16)
    mask_bf = absb("mask_bf", [128, 4, NE], BF16)
    B_maskbf = [Buf() for _ in range(4)]

    P0 = ctx.enter_context(nc.psum_tensor("P0", [128, 1024], F32))
    P1 = ctx.enter_context(nc.psum_tensor("P1", [128, 1024], F32))
    P2 = ctx.enter_context(nc.psum_tensor("P2", [128, 512], F32))
    P3 = ctx.enter_context(nc.psum_tensor("P3", [128, 1024], BF16))
    P4 = ctx.enter_context(nc.psum_tensor("P4", [128, 1024], F32))

    B_consts = Buf()
    B_ln = {k: [Buf() for _ in range(4)] for k in ("nm", "var", "rstd", "mr", "sq")}
    B_l = [Buf(), Buf()]
    B_r = [Buf(), Buf()]
    B_rt = {k: [Buf() for _ in range(4)] for k in
            ("lg", "lmax", "sub", "sgn", "pe", "den", "gate")}
    B_gate = [Buf() for _ in range(NT)]
    B_slot = Buf()
    B_maskT = [Buf() for _ in range(NT)]
    B_P = {id(P0): Buf(), id(P1): Buf(), id(P2): Buf(), id(P4): Buf()}
    B_P4h = [Buf(), Buf()]
    B_P3 = Buf()

    p.emit("sync", "dma_start", ident[:], ident_in, writes=[B_consts], dma="cst")
    p.emit("sync", "dma_start", maskd[:], maskd_in, writes=[B_consts], dma="cst")
    p.emit("sync", "dma_start", ones_sb[:], ones_in, writes=[B_consts], dma="cst")
    p.emit("sync", "dma_start", bo_sb[:], bo_b, writes=[B_consts], dma="cst")
    p.emit("sync", "dma_start", br_sb[:], br_b, writes=[B_consts], dma="cst")
    p.emit("vector", "tensor_copy", ones_bf[:], ones_sb[:],
           reads=[B_consts], writes=[B_consts], first=False)
    B_wr = Buf()
    p.emit("gpsimd", "dma_start", wr_bf[:],
           wr.rearrange("(dc p) e -> p dc e", p=128), writes=[B_wr], dma="wrld")

    def layer_norm(x_ap, out_ap, B_x, B_out, sl):
        nm, var, rstd = nm_sb[:, sl:sl + 1], var_sb[:, sl:sl + 1], rstd_sb[:, sl:sl + 1]
        mr = mr_sb[:, sl:sl + 1]
        sq = sq_sb[:, sl % 2, :]
        Bn, Bv, Br_, Bmr, Bsq = (B_ln["nm"][sl], B_ln["var"][sl], B_ln["rstd"][sl],
                                 B_ln["mr"][sl], B_ln["sq"][sl % 2])
        p.emit("vector", "reduce_sum", nm, x_ap, AX.X,
               reads=[B_x], writes=[Bn])
        p.emit("scalar", "activation", sq, x_ap, AF.Square, accum_out=var,
               reads=[B_x], writes=[Bv, Bsq])
        p.emit("vector", "tensor_scalar_mul", nm, nm, 1.0 / DM,
               reads=[Bn], writes=[Bn], first=False)
        p.emit("vector", "tensor_mul", mr, nm, nm,
               reads=[Bn], writes=[Bmr])
        p.emit("vector", "tensor_scalar", var, var, 1.0 / DM, EPS_LN,
               ALU.mult, ALU.add, reads=[Bv], writes=[Bv], first=False)
        p.emit("vector", "tensor_sub", var, var, mr,
               reads=[Bv, Bmr], writes=[Bv], first=False)
        p.emit("scalar", "activation", rstd, var, AF.Ln,
               reads=[Bv], writes=[Br_])
        p.emit("scalar", "activation", rstd, rstd, AF.Exp, scale=-0.5,
               reads=[Br_], writes=[Br_], first=False)
        p.emit("vector", "tensor_mul", mr, nm, rstd,
               reads=[Bn, Br_], writes=[Bmr])
        p.emit("vector", "tensor_scalar_mul", mr, mr, -1.0,
               reads=[Bmr], writes=[Bmr], first=False)
        p.emit("vector", "tensor_scalar", out_ap, x_ap, rstd, mr,
               ALU.mult, ALU.add,
               reads=[B_x, Br_, Bmr], writes=[B_out])
    tr_slot = [0]

    def transpose_tile(in_tile, out_ap, B_in, B_out):
        for dc in range(8):
            p.emit("tensor", "transpose", P3[:, dc * 128:(dc + 1) * 128],
                   in_tile[:, dc * 128:(dc + 1) * 128], ident[:],
                   reads=[B_in, B_consts], writes=[B_P3],
                   first=(dc == 0), declare_only=(dc < 7))
        p.emit("vector", "tensor_copy", out_ap, P3[:],
               reads=[B_P3], writes=[B_out])

    with ExitStack() as actx:
        asb = lambda *a: actx.enter_context(nc.sbuf_tensor(*a))
        wq_bf = asb("wq_bf", [128, 8, DM], BF16)
        wkv_bf = asb("wkv_bf", [128, 8, 2 * DM], BF16)
        x_g = asb("x_g", [128, 2, 4, DM], F32)
        xn_tm = asb("xn_tm", [128, 4, DM], BF16)
        xn_fm = asb("xn_fm", [128, 2, 4, 8, 128], BF16)
        q_stage = asb("q_stage", [128, 2, 8, 512], BF16)
        k_stage = asb("k_stage", [128, 2, 8, 512], BF16)
        v_stage = asb("v_stage", [128, 2, 4, 16 * 65], BF16)
        vtmp = asb("vtmp", [128, 1024], BF16)

        B_wq, B_wkv = [Buf() for _ in range(8)], [Buf() for _ in range(8)]
        B_xg = [[Buf() for _ in range(4)] for _ in range(2)]
        B_xntm = [Buf() for _ in range(4)]
        B_xnfm = [[Buf() for _ in range(4)] for _ in range(2)]
        B_qs = [[Buf() for _ in range(8)] for _ in range(2)]
        B_ks = [[Buf() for _ in range(8)] for _ in range(2)]
        B_vs = [[Buf() for _ in range(4)] for _ in range(2)]
        B_vtmp = Buf()

        for dc in range(8):
            p.emit("gpsimd", "dma_start", wq_bf[:, dc, :],
                   wq[dc * 128:(dc + 1) * 128, :], writes=[B_wq[dc]],
                   dma="wqkv", declare_only=True)
        for dc in range(8):
            p.emit("gpsimd", "dma_start", wkv_bf[:, dc, :],
                   wkv[dc * 128:(dc + 1) * 128, :], writes=[B_wkv[dc]],
                   dma="wqkv", declare_only=True)
        for dc in range(8):
            B_wq[dc].ready = [("D_wqkv", p.cnt["D_wqkv"])]
            B_wkv[dc].ready = [("D_wqkv", p.cnt["D_wqkv"])]

        B_vones = Buf()
        for b in range(2):
            for h in range(16):
                p.emit("vector", "memset",
                       v_stage[:, b, :, h * 65 + 64:h * 65 + 65], 1.0,
                       writes=[B_vones], inc=(b == 1 and h == 15), first=False)

        def emit_lntr(tg):
            bu = tg % 2
            p.emit("sync", "dma_start", x_g[:, bu, :, :],
                   x_in[tg * 512:(tg + 1) * 512, :].rearrange("(t p) d -> p t d", p=128),
                   writes=B_xg[bu], dma="xg%d" % bu, declare_only=True)
            for tt in range(4):
                B_xg[bu][tt].ready = [("D_xg%d" % bu, p.cnt["D_xg%d" % bu])]
                B_xg[bu][tt].free = []
            for tt in range(4):
                layer_norm(x_g[:, bu, tt, :], xn_tm[:, tt, :], B_xg[bu][tt],
                           B_xntm[tt], tt)
                transpose_tile(xn_tm[:, tt, :], xn_fm[:, bu, tt, :, :],
                               B_xntm[tt], B_xnfm[bu][tt])

        def emit_proj(tg):
            bu = tg % 2
            for kind in range(2):
                wbf, wB = (wq_bf, B_wq) if kind == 0 else (wkv_bf, B_wkv)
                stage, sB = (q_stage, B_qs) if kind == 0 else (k_stage, B_ks)
                for fc in range(8):
                    ps = P0 if fc % 2 == 0 else P1
                    pB = B_P[id(ps)]
                    rd_all = list(wB) + [B_xnfm[bu][t] for t in range(4)]
                    for dc in range(8):
                        first, last = dc == 0, dc == 7
                        rd = rd_all if last else [wB[dc]] + [B_xnfm[bu][t] for t in range(4)]
                        p.emit("tensor", "matmul", ps[:, 0:512],
                               wbf[:, dc, fc * 128:(fc + 1) * 128],
                               xn_fm[:, bu, :, dc, :],
                               start=first, stop=last,
                               reads=rd, writes=[pB],
                               inc=last, declare_only=first and not last, first=first)
                    if kind == 0:
                        p.emit("scalar", "mul", stage[:, bu, fc, :], ps[:, 0:512],
                               DH ** -0.5, reads=[pB], writes=[sB[bu][fc]])
                    else:
                        p.emit("scalar", "copy", stage[:, bu, fc, :], ps[:, 0:512],
                               reads=[pB], writes=[sB[bu][fc]])
            for tt in range(4):
                for half in range(2):
                    hB = B_P4h[half]
                    rd_all = list(B_wkv) + [B_xnfm[bu][tt]]
                    for dc in range(8):
                        first, last = dc == 0, dc == 7
                        p.emit("tensor", "matmul", P4[:, half * 512:(half + 1) * 512],
                               xn_fm[:, bu, tt, dc, :],
                               wkv_bf[:, dc, DM + half * 512:DM + (half + 1) * 512],
                               start=first, stop=last,
                               reads=rd_all if last else [B_wkv[dc], B_xnfm[bu][tt]],
                               writes=[hB],
                               inc=last, declare_only=first and not last, first=first)
                    p.emit("scalar", "copy", vtmp[:, half * 512:(half + 1) * 512],
                           P4[:, half * 512:(half + 1) * 512],
                           reads=[hB], writes=[B_vtmp], first=(half == 0),
                           declare_only=(half == 0))
                p.emit("vector", "tensor_copy",
                       v_stage[:, bu, tt, :].rearrange("p (h c) -> p h c", c=65)[:, :, 0:64],
                       vtmp[:].rearrange("p (h c) -> p h c", c=64),
                       reads=[B_vtmp], writes=[B_vs[bu][tt]])
            p.emit("scalar", "dma_start", q_hbm[tg], q_stage[:, bu, :, :],
                   reads=[B_qs[bu][fc] for fc in range(8)], dma="qst%d" % bu)
            p.emit("scalar", "dma_start",
                   k_hbm[:, :, tg * 512:(tg + 1) * 512].rearrange("fc p c -> p fc c"),
                   k_stage[:, bu, :, :],
                   reads=[B_ks[bu][fc] for fc in range(8)], dma="kst%d" % bu)
            p.emit("scalar", "dma_start",
                   v_hbm[tg * 4:(tg + 1) * 4].rearrange("kt p c -> p kt c"),
                   v_stage[:, bu, :, :],
                   reads=B_vs[bu] + [B_vones], dma="vst%d" % bu)

        for tg in range(NG):
            emit_lntr(tg)
            if tg >= 1:
                emit_proj(tg - 1)
        emit_proj(NG - 1)

        p.barrier("A->B")

    with ExitStack() as bctx:
        bsb = lambda *a: bctx.enter_context(nc.sbuf_tensor(*a))
        wo_bf = bsb("wo_bf", [128, 8, DM], BF16)
        v_sb = bsb("v_sb", [128, NT, 16 * 65], BF16)
        k_buf = bsb("k_buf", [128, 2, N], BF16)
        q_buf = bsb("q_buf", [128, 2, 512], BF16)
        pT_sb = bsb("pT_sb", [128, 4096], BF16)
        attn_fm = bsb("attn_fm", [128, 8, 512], BF16)
        attn_f32 = bsb("attn_f32", [128, 512], F32)
        tmp_bf = bsb("tmp_bf", [128, 512], BF16)
        xg2 = bsb("xg2", [128, 4, DM], F32)
        x2_sb = bsb("x2_sb", [128, 4, DM], F32)
        xn2_tm = bsb("xn2_tm", [128, 4, DM], BF16)
        xn2_fm = bsb("xn2_fm", [128, 4, 8, 128], BF16)

        B_wo = [Buf() for _ in range(8)]
        B_vc = [Buf() for _ in range(4)]
        B_kb = [Buf(), Buf()]
        B_qb = [Buf(), Buf()]
        B_pT = [Buf() for _ in range(8)]
        B_afm = [[Buf(), Buf()] for _ in range(8)]
        B_af32 = Buf()
        B_tmpbf = Buf()
        B_xg2 = [Buf() for _ in range(4)]
        B_x2 = [Buf() for _ in range(4)]
        B_xn2tm = [Buf() for _ in range(4)]
        B_xn2fm = [[Buf() for _ in range(4)] for _ in range(8)]

        for dc in range(8):
            p.emit("gpsimd", "dma_start", wo_bf[:, dc, :],
                   wo[dc * 128:(dc + 1) * 128, :], writes=[B_wo[dc]],
                   dma="wo", declare_only=True)
        for dc in range(8):
            B_wo[dc].ready = [("D_wo", p.cnt["D_wo"])]
        def load_kq(qc, fc):
            par = fc % 2
            nkb = 4 * qc + 4
            p.emit("sync", "dma_start", k_buf[:, par, 0:nkb * 128],
                   k_hbm[fc, :, 0:nkb * 128], writes=[B_kb[par]],
                   dma="kq%d" % par, declare_only=True)
            p.emit("sync", "dma_start", q_buf[:, par, :],
                   q_hbm[qc, :, fc, :], writes=[B_qb[par]],
                   dma="kq%d" % par, declare_only=True)
            tot = [("D_kq%d" % par, p.cnt["D_kq%d" % par])]
            for b in (B_kb[par], B_qb[par]):
                b.ready = list(tot)
                b.free = []

        load_kq(0, 0)
        VCH = max(1, NT // 4)
        for vc in range(4):
            lo, hi = vc * VCH, min((vc + 1) * VCH, NT)
            if lo >= hi:
                B_vc[vc].ready = list(B_vc[vc - 1].ready)
                continue
            p.emit("sync", "dma_start", v_sb[:, lo:hi, :],
                   v_hbm[lo:hi].rearrange("kt p c -> p kt c"),
                   writes=[B_vc[vc]], dma="vl%d" % vc)
        zero_sb = bsb("zero_sb", [128, DM], BF16)
        B_zero = Buf()
        B_zx = Buf()
        p.emit("vector", "memset", zero_sb[:], 0.0, writes=[B_zero])
        nzt = NE * CAP // 128 + 1
        for zi in range(nzt):
            p.emit("scalar", "dma_start", xs_hbm[zi * 128:(zi + 1) * 128, :],
                   zero_sb[:], reads=[B_zero], dma="zx", declare_only=True)
        p.emit("scalar", "dma_start", ys_hbm[NE * CAP:NE * CAP + 128, :],
               zero_sb[:], reads=[B_zero], dma="zx", declare_only=True)
        B_zx.ready = [("D_zx", p.cnt["D_zx"])]
        B_zx_holder[0] = B_zx

        ones8b_sb = bsb("ones8b_sb", [8, 1], F32)
        comb_sb = bsb("comb_sb", [8, 2], F32)
        cumc_sb = bsb("cumc_sb", [8, 2, 512], F32)
        slotf2_sb = bsb("slotf2_sb", [128, 2, 4], F32)
        B_ones8 = Buf()
        B_comb = [Buf(), Buf()]
        B_cumc = [Buf(), Buf()]
        B_slotf2 = [Buf(), Buf()]
        p.emit("sync", "dma_start", ones8b_sb[:], ones8_in, writes=[B_ones8],
               dma="r8", declare_only=True)
        p.emit("sync", "dma_start", comb_sb[:, 1:2], basem1_in,
               writes=[B_comb[1]], dma="r8", declare_only=True)
        tot_r8 = [("D_r8", p.cnt["D_r8"])]
        B_ones8.ready = list(tot_r8)
        B_comb[1].ready = list(tot_r8)

        pend_epi = [None]
        pend_osw = [None]
        pend_slt = [None]
        for qc in range(NG):
            nkb = 4 * qc + 4
            p.emit("gpsimd", "dma_start", xg2[:],
                   x_in[qc * 512:(qc + 1) * 512, :].rearrange("(t p) d -> p t d", p=128),
                   writes=B_xg2, dma="xb0", declare_only=True)
            for tt in range(4):
                B_xg2[tt].ready = [("D_xb0", p.cnt["D_xb0"])]
                B_xg2[tt].free = []
            for fc in range(8):
                par = fc % 2
                nfc, nqc = fc + 1, qc
                if nfc == 8:
                    nfc, nqc = 0, qc + 1
                if nqc < NG:
                    load_kq(nqc, nfc)
                SPS = (P0, P1)
                nkp = nkb // 2

                def emit_s(h2, kb, j):
                    ps = SPS[h2]
                    p.emit("tensor", "matmul", ps[:, j * 512:(j + 1) * 512],
                           k_buf[h2 * 64:h2 * 64 + 64, par, kb * 128:(kb + 1) * 128],
                           q_buf[h2 * 64:h2 * 64 + 64, par, :],
                           start=True, stop=True,
                           reads=[B_kb[par], B_qb[par]], writes=[B_P[id(ps)]],
                           inc=(j == 1), declare_only=(j == 0), first=(j == 0))

                def emit_exp(h2, pp):
                    ps = SPS[h2]
                    s0 = 4 * h2 + (2 * pp) % 4
                    p.emit("scalar", "activation",
                           pT_sb[:, s0 * 512:(s0 + 2) * 512], ps[:], AF.Exp,
                           reads=[B_P[id(ps)]], writes=[B_pT[s0], B_pT[s0 + 1]])
                    for j in range(2):
                        kb = 2 * pp + j
                        d = kb - 4 * qc
                        if d >= 0:
                            sl = 4 * h2 + kb % 4
                            p.emit("vector", "tensor_mul",
                                   pT_sb[:, sl * 512:(sl + 1) * 512],
                                   pT_sb[:, sl * 512:(sl + 1) * 512],
                                   maskd[:, d, :],
                                   reads=[B_pT[sl], B_consts],
                                   writes=[B_pT[sl]], first=False)

                pv_ps_l = (P2[0:65, :], P4[0:65, 0:512])
                pv_B_l = (B_P[id(P2)], B_P4h[0])

                def emit_pv(h2, kb):
                    sl = 4 * h2 + kb % 4
                    first, last = kb == 0, kb == nkb - 1
                    odd = kb % 2 == 1
                    rd = [B_vc[min(3, kb // VCH)], B_pT[sl]]
                    if odd:
                        rd.append(B_pT[4 * h2 + (kb - 1) % 4])
                    p.emit("tensor", "matmul", pv_ps_l[h2],
                           v_sb[:, kb, (2 * fc + h2) * 65:(2 * fc + h2) * 65 + 65],
                           pT_sb[:, sl * 512:(sl + 1) * 512],
                           start=first, stop=last,
                           reads=rd, writes=[pv_B_l[h2]],
                           inc=odd, declare_only=not last, first=first)

                for pp in range(nkp):
                    for j in range(2):
                        for h2 in range(2):
                            emit_s(h2, 2 * pp + j, j)
                    for h2 in range(2):
                        emit_exp(h2, pp)
                    if pp == 0 and fc == 2 and pend_osw[0] is not None:
                        pend_osw[0]()
                        pend_osw[0] = None
                    if pp == 0 and fc == 4 and pend_slt[0] is not None:
                        pend_slt[0]()
                        pend_slt[0] = None
                    if pp == 1 and pend_epi[0] is not None:
                        pend_epi[0]()
                        pend_epi[0] = None
                    if pp >= 1:
                        for h2 in range(2):
                            emit_pv(h2, 2 * (pp - 1))
                            emit_pv(h2, 2 * (pp - 1) + 1)
                for h2 in range(2):
                    emit_pv(h2, nkb - 2)
                    emit_pv(h2, nkb - 1)

                for h2 in range(2):
                    lc = slice(h2 * 512, h2 * 512 + 512)
                    p.emit("vector", "tensor_scalar_add", lr_sb[64:65, lc],
                           pv_ps_l[h2][64:65, :], EPS_ATT,
                           reads=[pv_B_l[h2]], writes=[B_l[h2]])
                    p.emit("scalar", "activation", lr_sb[64:65, lc],
                           lr_sb[64:65, lc], AF.Ln,
                           reads=[B_l[h2]], writes=[B_l[h2]], first=False)
                    p.emit("scalar", "activation", r_bf[64:65, lc],
                           lr_sb[64:65, lc], AF.Exp, scale=-1.0,
                           reads=[B_l[h2]], writes=[B_r[h2]])

                def make_epilogue(fc=fc):
                    def epi():
                        for h2 in range(2):
                            lc = slice(h2 * 512, h2 * 512 + 512)
                            p.emit("tensor", "matmul", P4[0:64, 512:1024],
                                   ones_bf[64:65, :], r_bf[64:65, lc],
                                   start=True, stop=True,
                                   reads=[B_r[h2], B_consts], writes=[B_P4h[1]])
                            p.emit("vector", "tensor_copy", attn_f32[0:64, :],
                                   pv_ps_l[h2][0:64, :],
                                   reads=[pv_B_l[h2]], writes=[B_af32])
                            if h2 == 0:
                                p.emit("vector", "tensor_mul",
                                       attn_fm[0:64, fc, :],
                                       attn_f32[0:64, :], P4[0:64, 512:1024],
                                       reads=[B_af32, B_P4h[1]],
                                       writes=[B_afm[fc][0]])
                            else:
                                p.emit("vector", "tensor_mul", tmp_bf[0:64, :],
                                       attn_f32[0:64, :], P4[0:64, 512:1024],
                                       reads=[B_af32, B_P4h[1]], writes=[B_tmpbf])
                                p.emit("tensor", "matmul", P4[64:128, 0:512],
                                       ident[0:64, 0:64], tmp_bf[0:64, :],
                                       start=True, stop=True, tile_position=(0, 64),
                                       reads=[B_tmpbf, B_consts], writes=[B_P4h[0]])
                                p.emit("vector", "tensor_copy",
                                       attn_fm[64:128, fc, :], P4[64:128, 0:512],
                                       reads=[B_P4h[0]], writes=[B_afm[fc][1]])
                    return epi
                pend_epi[0] = make_epilogue()
            if pend_epi[0] is not None:
                pend_epi[0]()
                pend_epi[0] = None
            for qt in range(4):
                for half in range(2):
                    hB = B_P4h[half]
                    rd_all = list(B_wo) + [B_afm[f][j] for f in range(8) for j in range(2)]
                    for fc in range(8):
                        first, last = fc == 0, fc == 7
                        p.emit("tensor", "matmul", P4[:, half * 512:(half + 1) * 512],
                               attn_fm[:, fc, qt * 128:(qt + 1) * 128],
                               wo_bf[:, fc, half * 512:(half + 1) * 512],
                               start=first, stop=last,
                               reads=rd_all if last else
                               [B_wo[fc], B_afm[fc][0], B_afm[fc][1]],
                               writes=[hB],
                               inc=last, declare_only=first and not last, first=first)
                    sl = slice(half * 512, (half + 1) * 512)
                    p.emit("vector", "tensor_add", x2_sb[:, qt, sl],
                           P4[:, sl], bo_sb[:, sl],
                           reads=[hB, B_consts], writes=[B_x2[qt]])
                    p.emit("vector", "tensor_add", x2_sb[:, qt, sl],
                           x2_sb[:, qt, sl], xg2[:, qt, sl],
                           reads=[B_xg2[qt], B_x2[qt]],
                           writes=[B_x2[qt]], first=False)
                layer_norm(x2_sb[:, qt, :], xn2_tm[:, qt, :], B_x2[qt],
                           B_xn2tm[qt], qt)
            def make_osweep3(qc=qc):
                def osw3():
                    for qt in range(4):
                        transpose_tile(xn2_tm[:, qt, :], xn2_fm[:, qt, :, :],
                                       B_xn2tm[qt], B_xn2fm[0][qt])
                        rt_B = B_P4h[1]
                        rt_rd = [B_xn2fm[0][qt], B_wr]
                        for dc in range(8):
                            first, last = dc == 0, dc == 7
                            p.emit("tensor", "matmul", P4[:, 512:520],
                                   xn2_fm[:, qt, dc, :],
                                   wr_bf[:, dc, :],
                                   start=first, stop=last,
                                   reads=rt_rd if last else [B_xn2fm[0][qt], B_wr],
                                   writes=[rt_B],
                                   inc=last, declare_only=first and not last,
                                   first=first)
                        p.emit("vector", "tensor_add", lg_sb[:, qt, :],
                               P4[:, 512:520], br_sb[:],
                               reads=[rt_B, B_consts], writes=[B_rt["lg"][qt]])
                        p.emit("vector", "reduce_max", lmax_sb[:, qt:qt + 1],
                               lg_sb[:, qt, :], AX.X,
                               reads=[B_rt["lg"][qt]], writes=[B_rt["lmax"][qt]])
                        p.emit("vector", "tensor_scalar", sub_sb[:, qt, :],
                               lg_sb[:, qt, :], lmax_sb[:, qt:qt + 1], None,
                               ALU.subtract,
                               reads=[B_rt["lg"][qt], B_rt["lmax"][qt]],
                               writes=[B_rt["sub"][qt]])
                        p.emit("scalar", "sign", sgn_sb[:, qt, :], sub_sb[:, qt, :],
                               reads=[B_rt["sub"][qt]], writes=[B_rt["sgn"][qt]])
                        p.emit("scalar", "activation", pe_sb[:, qt, :],
                               sub_sb[:, qt, :], AF.Exp,
                               reads=[B_rt["sub"][qt]], writes=[B_rt["pe"][qt]])
                        p.emit("vector", "reduce_sum", den_sb[:, qt:qt + 1],
                               pe_sb[:, qt, :], AX.X,
                               reads=[B_rt["pe"][qt]], writes=[B_rt["den"][qt]])
                        ti = qc * 4 + qt
                        p.emit("vector", "reciprocal", gates_sb[:, ti:ti + 1],
                               den_sb[:, qt:qt + 1],
                               reads=[B_rt["den"][qt]], writes=[B_gate[ti]])
                        p.emit("vector", "tensor_scalar_add", mask_bf[:, qt, :],
                               sgn_sb[:, qt, :], 1.0,
                               reads=[B_rt["sgn"][qt]], writes=[B_maskbf[qt]])
                        s = tr_slot[0] % 8
                        tr_slot[0] += 1
                        p.emit("tensor", "transpose", P3[0:8, s * 128:(s + 1) * 128],
                               mask_bf[:, qt, :], ident[:],
                               reads=[B_maskbf[qt], B_consts], writes=[B_P3])
                        p.emit("vector", "tensor_copy",
                               maskT_sb[0:8, ti * 128:(ti + 1) * 128],
                               P3[0:8, s * 128:s * 128 + 128],
                               reads=[B_P3], writes=[B_maskT[ti]])
                return osw3
            pend_osw[0] = make_osweep3()

            def make_slotchain(qc=qc):
                def slt():
                    cc = qc % 2
                    prev = 1 - cc
                    mcols = maskT_sb[0:8, qc * 512:(qc + 1) * 512]
                    cum = cumc_sb[0:8, cc, :]
                    mrd = [B_maskT[qc * 4 + i] for i in range(4)]
                    p.emit("vector", "tensor_tensor_scan", cum, mcols, mcols,
                           0.0, ALU.add, ALU.bypass,
                           reads=mrd, writes=[B_cumc[cc]])
                    p.emit("vector", "tensor_scalar_add", cum, cum,
                           comb_sb[0:8, prev:prev + 1],
                           reads=[B_cumc[cc], B_comb[prev]],
                           writes=[B_cumc[cc]], first=False)
                    p.emit("vector", "tensor_copy", comb_sb[0:8, cc:cc + 1],
                           cumc_sb[0:8, cc, 511:512],
                           reads=[B_cumc[cc]], writes=[B_comb[cc]])
                    p.emit("vector", "tensor_mul", cum, cum, mcols,
                           reads=[B_cumc[cc]] + mrd,
                           writes=[B_cumc[cc]], first=False)
                    for qt in range(4):
                        p.emit("tensor", "matmul", P4[:, 520 + qt:521 + qt],
                               cumc_sb[0:8, cc, qt * 128:(qt + 1) * 128],
                               ones8b_sb[:],
                               start=True, stop=True,
                               reads=[B_cumc[cc], B_ones8], writes=[B_P4h[1]],
                               inc=(qt == 3), declare_only=(qt < 3),
                               first=(qt == 0))
                    p.emit("vector", "tensor_scalar_min", slotf2_sb[:, cc, :],
                           P4[:, 520:524], float(NE * CAP),
                           reads=[B_P4h[1]], writes=[B_slotf2[cc]])
                    p.emit("vector", "tensor_copy",
                           slot_sb[:, qc * 4:qc * 4 + 4], slotf2_sb[:, cc, :],
                           reads=[B_slotf2[cc]], writes=[B_slot])
                    for qt in range(4):
                        ti = qc * 4 + qt
                        p.emit("gpsimd", "indirect_dma_start", xs_hbm,
                               IndirectOffsetOnAxis(slot_sb[:, ti:ti + 1], 0),
                               xn2_tm[:, qt, :], None,
                               reads=[B_slot, B_xn2tm[qt], B_zx],
                               dma="sc%d" % qt)
                return slt
            pend_slt[0] = make_slotchain()
            p.emit("scalar", "dma_start",
                   x2_hbm[qc * 512:(qc + 1) * 512, :].rearrange("(t p) d -> p t d", p=128),
                   x2_sb[:], reads=B_x2, dma="x2st")

        if pend_osw[0] is not None:
            pend_osw[0]()
            pend_osw[0] = None
        if pend_slt[0] is not None:
            pend_slt[0]()
            pend_slt[0] = None
        p.barrier("B->C")

    ab_ctx.close()

    CGRP = [(o, min(512, CAP - o)) for o in range(0, CAP, 512)]
    NTE = CAP // 128
    with ExitStack() as cctx:
        csb = lambda *a: cctx.enter_context(nc.sbuf_tensor(*a))
        w1_bf = csb("w1_bf", [128, 2, 8, HFF], BF16)
        w2_bf = csb("w2_bf", [128, 16, DM], BF16)
        xsf_sb = csb("xsf_sb", [128, 3, 8, 512], BF16)
        h_buf = csb("h_buf", [128, 16, 512], BF16)
        y_tmp = csb("y_tmp", [128, DM], F32)
        ys_bf = csb("ys_bf", [128, 2, DM], BF16)
        b1_sb = csb("b1_sb", [128, 16], F32)
        b2_sb = csb("b2_sb", [128, DM], F32)

        B_w1 = [[Buf() for _ in range(8)] for _ in range(2)]
        B_w2 = [Buf() for _ in range(16)]
        B_xsf = [Buf(), Buf(), Buf()]
        B_h = [Buf() for _ in range(16)]
        B_yt = Buf()
        B_ysb = [Buf(), Buf()]
        B_b1 = Buf()
        B_b2 = Buf()

        def load_expert(e):
            wp = e % 2
            k1, k2 = "w1e%d" % wp, "w2e%d" % wp
            for dc in range(8):
                p.emit("gpsimd", "dma_start", w1_bf[:, wp, dc, :],
                       w1[e, dc * 128:(dc + 1) * 128, :], writes=[B_w1[wp][dc]],
                       dma=k1, declare_only=True)
            p.emit("gpsimd", "dma_start", b1_sb[:], b1_l[e], writes=[B_b1],
                   dma=k1, declare_only=True)
            tot1 = [("D_" + k1, p.cnt["D_" + k1])]
            for dc in range(8):
                B_w1[wp][dc].ready = list(tot1)
            B_b1.ready = list(tot1)
            for hc in range(16):
                p.emit("gpsimd", "dma_start", w2_bf[:, hc, :],
                       w2[e, hc * 128:(hc + 1) * 128, :], writes=[B_w2[hc]],
                       dma=k2, declare_only=True)
            p.emit("gpsimd", "dma_start", b2_sb[:], b2_b[e], writes=[B_b2],
                   dma=k2, declare_only=True)
            tot2 = [("D_" + k2, p.cnt["D_" + k2])]
            for hc in range(16):
                B_w2[hc].ready = list(tot2)
            B_b2.ready = list(tot2)

        xs_count = [0]
        ys_count = [0]

        def load_xs(e, gi):
            off, ln = CGRP[gi]
            bu = xs_count[0] % 3
            xs_count[0] += 1
            for dc in range(8):
                p.emit("sync", "dma_start_transpose",
                       xsf_sb[:, bu, dc, 0:ln],
                       xs_hbm[e * CAP + off:e * CAP + off + ln,
                              dc * 128:(dc + 1) * 128],
                       writes=[B_xsf[bu]], dma="xf%d" % bu, declare_only=True)
            B_xsf[bu].ready = [("D_xf%d" % bu, p.cnt["D_xf%d" % bu])]
            B_xsf[bu].free = []

        NGRP = len(CGRP)
        load_expert(0)
        load_xs(0, 0)
        if NGRP > 1:
            load_xs(0, 1)
        else:
            load_xs(1, 0)
        xs_use = [0]
        for e in range(NE):
            for gi, (off, ln) in enumerate(CGRP):
                bu = xs_use[0] % 3
                xs_use[0] += 1
                g2 = e * NGRP + gi + 2
                if g2 < NE * NGRP:
                    load_xs(g2 // NGRP, g2 % NGRP)
                wp = e % 2
                for hc in range(16):
                    ps = P0 if hc % 2 == 0 else P1
                    pB = B_P[id(ps)]
                    h_rd = list(B_w1[wp]) + [B_xsf[bu]]
                    for dc in range(8):
                        first, last = dc == 0, dc == 7
                        p.emit("tensor", "matmul", ps[:, 0:ln],
                               w1_bf[:, wp, dc, hc * 128:(hc + 1) * 128],
                               xsf_sb[:, bu, dc, 0:ln],
                               start=first, stop=last,
                               reads=h_rd if last else [B_w1[wp][dc], B_xsf[bu]],
                               writes=[pB],
                               inc=last, declare_only=first and not last,
                               first=first)
                    p.emit("scalar", "activation", h_buf[:, hc, 0:ln],
                           ps[:, 0:ln], AF.Relu, bias=b1_sb[:, hc:hc + 1],
                           reads=[pB, B_b1], writes=[B_h[hc]])
                for qt in range(ln // 128):
                    bu2 = ys_count[0] % 2
                    ys_count[0] += 1
                    for half in range(2):
                        hB = B_P4h[half]
                        y_rd = list(B_h) + list(B_w2)
                        for hc in range(16):
                            first, last = hc == 0, hc == 15
                            p.emit("tensor", "matmul",
                                   P4[:, half * 512:(half + 1) * 512],
                                   h_buf[:, hc, qt * 128:(qt + 1) * 128],
                                   w2_bf[:, hc, half * 512:(half + 1) * 512],
                                   start=first, stop=last,
                                   reads=y_rd if last else [B_h[hc], B_w2[hc]],
                                   writes=[hB],
                                   inc=last, declare_only=first and not last,
                                   first=first)
                        sl = slice(half * 512, (half + 1) * 512)
                        p.emit("vector", "tensor_add", y_tmp[:, sl],
                               P4[:, sl], b2_sb[:, sl],
                               reads=[hB, B_b2], writes=[B_yt],
                               first=(half == 0))
                    p.emit("scalar", "copy", ys_bf[:, bu2, :], y_tmp[:],
                           reads=[B_yt], writes=[B_ysb[bu2]])
                    row = e * CAP + off + qt * 128
                    p.emit("scalar", "dma_start",
                           ys_hbm[row:row + 128, :],
                           ys_bf[:, bu2, :], reads=[B_ysb[bu2]], dma="yso%d" % bu2)
            if e + 1 < NE:
                load_expert(e + 1)

        ys_done = [(s, v) for s, v in p.cnt.items() if s.startswith("D_yso")]

        with ExitStack() as fctx:
            fsb = lambda *a: fctx.enter_context(nc.sbuf_tensor(*a))
            x2g = fsb("x2g", [128, 4, DM], F32)
            y_stage = fsb("y_stage", [128, 4, DM], F32)
            yg_sb = fsb("yg_sb", [128, 2, DM], BF16)
            ygs_sb = fsb("ygs_sb", [128, 2, DM], F32)
            B_x2g = Buf()
            B_ys = Buf()
            B_yg = [Buf(), Buf()]
            B_ygs = [Buf(), Buf()]
            for qc in range(NG):
                p.emit("sync", "dma_start", x2g[:],
                       x2_hbm[qc * 512:(qc + 1) * 512, :].rearrange(
                           "(t p) d -> p t d", p=128),
                       writes=[B_x2g], dma="x2g0")
                for qt in range(4):
                    ti = qc * 4 + qt
                    tb = ti % 2
                    p.emit("gpsimd", "indirect_dma_start", yg_sb[:, tb, :], None,
                           ys_hbm, IndirectOffsetOnAxis(slot_sb[:, ti:ti + 1], 0),
                           reads=[B_slot], writes=[B_yg[tb]], dma="gb%d" % tb,
                           extra=ys_done)
                    p.emit("scalar", "activation", ygs_sb[:, tb, :],
                           yg_sb[:, tb, :], AF.Copy,
                           scale=gates_sb[:, ti:ti + 1],
                           reads=[B_yg[tb], B_gate[ti]], writes=[B_ygs[tb]])
                    p.emit("vector", "tensor_add", y_stage[:, qt, :],
                           x2g[:, qt, :], ygs_sb[:, tb, :],
                           reads=[B_x2g, B_ygs[tb]], writes=[B_ys],
                           first=(qt == 0), declare_only=(qt < 3))
                p.emit("scalar", "dma_start",
                       y_out[qc * 512:(qc + 1) * 512, :].rearrange(
                           "(t p) d -> p t d", p=128),
                       y_stage[:], reads=[B_ys], dma="yst0")

    final_waits = [(s, v) for s, v in p.cnt.items() if s.startswith("D_")]
    p.wait_only("sync", final_waits)
    p.wait_only("scalar", final_waits)

    with ctx:
        sems = {}
        with ExitStack() as sctx:
            for name in p.cnt:
                sems[name] = sctx.enter_context(nc.semaphore(name))
            with nc.Block() as block:
                p.realize(block, sems)
    return nc


_NC_CACHE = {}


def _get_nc(debug=False):
    key = bool(debug)
    if key not in _NC_CACHE:
        _NC_CACHE[key] = build_program(debug=debug)
    return _NC_CACHE[key]


def _prep_inputs(inputs):
    g = {k: np.asarray(v) for k, v in inputs.items()}
    x = np.ascontiguousarray(g["L0_attn_x"], dtype=np.float32)
    wq = np.ascontiguousarray(g["L0_attn_Wq"], dtype=np.float32)
    wkv = np.ascontiguousarray(g["L0_attn_Wkv"], dtype=np.float32)
    wo = np.ascontiguousarray(g["L0_attn_Wo"], dtype=np.float32)
    bo = np.ascontiguousarray(g["L0_attn_bo"], dtype=np.float32)
    wr = np.ascontiguousarray(g["L0_ffn_Wr"], dtype=np.float32)
    br = np.ascontiguousarray(g["L0_ffn_br"], dtype=np.float32)
    w1 = np.ascontiguousarray(g["L0_ffn_W1"], dtype=np.float32)
    b1 = np.ascontiguousarray(g["L0_ffn_b1"], dtype=np.float32)
    w2 = np.ascontiguousarray(g["L0_ffn_W2"], dtype=np.float32)
    b2 = np.ascontiguousarray(g["L0_ffn_b2"], dtype=np.float32)

    bo_b = np.ascontiguousarray(np.broadcast_to(bo[None, :], (128, DM)))
    br_b = np.ascontiguousarray(np.broadcast_to(br[None, :], (128, NE)))
    b1_l = np.ascontiguousarray(b1.reshape(NE, 16, 128).transpose(0, 2, 1))
    b2_b = np.ascontiguousarray(np.broadcast_to(b2[:, None, :], (NE, 128, DM)))

    kk = np.arange(128)[:, None]
    qq = np.arange(512)[None, :]
    maskd = np.stack([(qq - 128 * d >= kk) for d in range(4)], axis=1)
    maskd = maskd.astype(ml_dtypes.bfloat16)
    ident = np.eye(128, dtype=ml_dtypes.bfloat16)
    ones_b = np.ones((128, 64), dtype=np.float32)
    ones8 = np.ones((8, 1), dtype=np.float32)
    cap = max(256, ((N * 5 // 16) + 127) // 128 * 128)
    basem1 = (np.arange(NE, dtype=np.float32)[:, None] * cap) - 1.0

    shared = dict(wq=wq, wkv=wkv, wo=wo, bo_b=bo_b, wr=wr, br_b=br_b,
                  w1=w1, b1_l=b1_l, w2=w2, b2_b=b2_b,
                  maskd=np.ascontiguousarray(maskd), ident=ident, ones_b=ones_b,
                  ones8=ones8, basem1=basem1)
    in_maps = []
    for b in range(8):
        m = dict(shared)
        m["x_in"] = np.ascontiguousarray(x[b])
        in_maps.append(m)
    return in_maps


def kernel(**inputs):
    in_maps = _prep_inputs(inputs)
    nc = _get_nc(debug=False)
    res = run_bass_kernel_spmd(nc, in_maps, core_ids=list(range(8)))
    out = np.stack([np.asarray(r["y_out"], dtype=np.float32) for r in res.results])
    return out
